# version 02
# baseline (speedup 1.0000x reference)
import numpy as np
import concourse.bass as bass
import concourse.mybir as mybir
from concourse.bass_utils import run_bass_kernel_spmd

F32 = mybir.dt.float32
BF16 = mybir.dt.bfloat16
AF = mybir.ActivationFunctionType
ALU = mybir.AluOpType
AXX = mybir.AxisListType.X

B = 2
SEQ = 4096
DM = 2048
DIN = 4096
NH = 64
PD = 64
DST = 128
MCH = 256
CONVD = DIN + 2 * DST
DIP = 2 * DIN + 2 * DST + NH
DFF = 8192
AH = 16
AHD = 128
ACH = 64
TOK = 1024
NCK = TOK // MCH
FT = DM // 128
ZT = DIN // 128
XBT = CONVD // 128
OT = 67
OTW = [128] * 66 + [64]
HP = 32
W1027 = 1027
LNEPS = 1e-5
MEPS = 1e-5
L2EPS = 1e-6
NPAIR = 4
AKT = SEQ // 128
ANC = SEQ // ACH



class Sem:
    def __init__(self, nc, name):
        self.h = nc.alloc_semaphore(name=name)
        self.n = 0

    def inc(self, instr, k=1):
        instr.then_inc(self.h, k)
        self.n += k
        return instr


def seq(eng, sem, instr):
    sem.inc(instr)
    eng.wait_ge(sem.h, sem.n)
    return instr


class Ctx:
    def __init__(self, nc):
        self.nc = nc
        self.sems = {}
        self.taps = {}
        self.tap_src = {}
        self.tap_requests = []
        self._jid = 0

    def sem(self, name):
        if name not in self.sems:
            self.sems[name] = Sem(self.nc, name)
        return self.sems[name]


def join(ctx, producers=None):
    with ctx.nc.Block():
        pass



def _host_consts():
    c = {}
    c["ident_f32"] = np.eye(128, dtype=np.float32)
    c["ones_row_f32"] = np.ones((1, 128), dtype=np.float32)
    c["ones_col_f32"] = np.ones((128, 1), dtype=np.float32)
    s_ = np.arange(128)[:, None]
    l_ = np.arange(256)[None, :]
    c["mask01_mamba"] = (s_ <= l_).astype(np.float32)
    jq = np.arange(512)[None, :]
    kt = np.arange(128)[:, None]
    c["mask_attn"] = np.ascontiguousarray(np.concatenate(
        [(kt <= (jq - 128 * d)).astype(np.float32) for d in range(4)], axis=1))
    cc_ = np.arange(64)[:, None]
    rr = np.arange(64)[None, :]
    blk = -((rr > cc_).astype(np.float32))
    m2 = np.zeros((128, 128), dtype=np.float32)
    m2[:64, :64] = blk
    m2[64:, 64:] = blk
    c["neg_striu2"] = m2
    i2 = np.zeros((128, 128), dtype=np.float32)
    i2[:64, :64] = np.eye(64)
    i2[64:, 64:] = np.eye(64)
    c["ident64x2"] = i2
    ind = np.zeros((2, 128), dtype=np.float32)
    ind[0, :64] = 1.0
    ind[1, 64:] = 1.0
    c["indicator2"] = ind
    inv = 1.0 / (10000.0 ** (np.arange(0, AHD, 2, dtype=np.float64) / AHD))
    th = np.arange(SEQ, dtype=np.float64)[:, None] * inv[None, :]
    import ml_dtypes
    cosT = np.cos(th).T.astype(np.float32)
    sinT = np.sin(th).T.astype(np.float32)
    c["rope_a"] = np.ascontiguousarray(
        np.concatenate([cosT, cosT], axis=0)).astype(ml_dtypes.bfloat16)
    c["rope_b"] = np.ascontiguousarray(
        np.concatenate([-sinT, sinT], axis=0)).astype(ml_dtypes.bfloat16)
    sw = np.zeros((128, 128), dtype=np.float32)
    sw[np.arange(128), (np.arange(128) + 64) % 128] = 1.0
    c["swap128"] = sw
    misc = np.zeros((128, 8), dtype=np.float32)
    misc[:, 0] = LNEPS
    misc[:, 1] = L2EPS
    misc[:, 2] = 1.0
    misc[:, 3] = 1e-30
    misc[:, 4] = MEPS
    c["misc"] = misc
    c["ones_full"] = np.ones((128, 256), dtype=np.float32)
    return c


def _host_prep(inputs):
    u = np.asarray(inputs["L0_attn_a_u"], dtype=np.float32)
    Wip = np.asarray(inputs["L0_attn_a_in_proj_w"], np.float32)
    convw = np.asarray(inputs["L0_attn_a_conv_w"], np.float32)
    convb = np.asarray(inputs["L0_attn_a_conv_b"], np.float32)
    dt_bias = np.asarray(inputs["L0_attn_a_dt_bias"], np.float32)
    A_log = np.asarray(inputs["L0_attn_a_A_log"], np.float32)
    D_skip = np.asarray(inputs["L0_attn_a_D_skip"], np.float32)
    norm_w = np.asarray(inputs["L0_attn_a_norm_w"], np.float32)
    Wout = np.asarray(inputs["L0_attn_a_out_proj_w"], np.float32)
    Wq = np.asarray(inputs["L0_attn_b_Wq"], np.float32)
    Wk = np.asarray(inputs["L0_attn_b_Wk"], np.float32)
    Wv = np.asarray(inputs["L0_attn_b_Wv"], np.float32)
    Wo = np.asarray(inputs["L0_attn_b_Wo"], np.float32)
    Wb = np.asarray(inputs["L0_attn_b_Wb"], np.float32)
    W0 = np.asarray(inputs["L0_ffn_W0"], np.float32)
    b0 = np.asarray(inputs["L0_ffn_bias0"], np.float32)
    W1 = np.asarray(inputs["L0_ffn_W1"], np.float32)
    b1 = np.asarray(inputs["L0_ffn_bias1"], np.float32)

    shared = {}
    shared["WipT"] = np.ascontiguousarray(Wip.T)
    convd = np.zeros((XBT, 4, 128, 128), dtype=np.float32)
    idx = np.arange(128)
    for j in range(XBT):
        for k in range(4):
            convd[j, k, idx, idx] = convw[128 * j: 128 * j + 128, k]
    shared["convdiag"] = convd
    shared["convb_r"] = np.ascontiguousarray(convb.reshape(XBT, 128).T)
    shared["dt_bias"] = dt_bias.reshape(64, 1).copy()
    shared["A_log"] = A_log.reshape(64, 1).copy()
    dsk = np.zeros((128, HP), dtype=np.float32)
    for hp in range(HP):
        dsk[:64, hp] = D_skip[2 * hp]
        dsk[64:, hp] = D_skip[2 * hp + 1]
    shared["Dskip_e"] = dsk
    shared["normw_r"] = np.ascontiguousarray(norm_w.reshape(ZT, 128).T)
    shared["WoutT"] = np.ascontiguousarray(Wout.T)
    shared["W0T"] = np.ascontiguousarray(W0.T)
    shared["W1T"] = np.ascontiguousarray(W1.T)
    shared["b0_r"] = np.ascontiguousarray(b0.reshape(DFF // 128, 128).T)
    shared["b1_r"] = np.ascontiguousarray(b1.reshape(FT, 128).T)
    shared.update(_host_consts())

    in_maps = []
    for c in range(8):
        b = c // 4
        t0 = (c % 4) * TOK
        m = dict(shared)
        if t0 > 0:
            halo = u[b, t0 - 3:t0, :]
        else:
            halo = np.zeros((3, DM), dtype=np.float32)
        ufm = np.concatenate([halo, u[b, t0:t0 + TOK, :]], axis=0).T
        m["u_fm"] = np.ascontiguousarray(ufm)
        h0 = (c % 4) * 4
        cols = slice(h0 * AHD, (h0 + 4) * AHD)
        m["Wq_sl"] = np.ascontiguousarray(Wq[:, cols])
        m["Wk_sl"] = np.ascontiguousarray(Wk[:, cols])
        m["Wv_sl"] = np.ascontiguousarray(Wv[:, cols])
        m["Wb_sl"] = np.ascontiguousarray(Wb[:, h0:h0 + 4])
        m["Wo_sl"] = np.ascontiguousarray(Wo[h0 * AHD:(h0 + 4) * AHD, :])
        g = c % 4
        msk = np.zeros((128, 8), dtype=np.float32)
        for j in range(4):
            msk[:, j] = 1.0 if j < g else 0.0
            msk[:, 4 + j] = 0.0 if j < g else 1.0
        m["gmask"] = msk
        in_maps.append(m)
    return in_maps


C_IDENT = 0
C_ONEROW = 128
C_MISC = 256
C_A = 264
C_DTB = 265
C_EXPA = 266
C_MASKM = 268
C_DSK = 524
C_NRMW = 556
C_CVB = 588
C_I64 = 624
C_STRIU = 752
C_IND2 = 880
C_ONESF = 1008
C_ALOG = 1264
C_GMASK = 1265
C_GMASK1 = 1269
CONSTW = 1344

CB_ONECOL = 0
CB_IDENT = 8
CB_IND2 = 136
CB_SWAP = 264
CBFW = 392


def build_program(tap_names=(), upto="all"):
    nc = bass.Bass("TRN2", target_bir_lowering=False)
    ctx = Ctx(nc)
    ctx.tap_requests = list(tap_names)
    ctx.upto = upto

    def di(name, shape, dt=F32):
        return nc.dram_tensor(name, list(shape), dt, kind="ExternalInput").ap()

    I = {}
    for name, shape in [
        ("rope_a", (128, SEQ)), ("rope_b", (128, SEQ))]:
        I[name] = di(name, shape, BF16)
    for name, shape in [
        ("u_fm", (DM, W1027)), ("WipT", (DM, DIP)), ("convdiag", (XBT, 4, 128, 128)),
        ("convb_r", (128, XBT)), ("dt_bias", (64, 1)), ("A_log", (64, 1)),
        ("Dskip_e", (128, HP)), ("normw_r", (128, ZT)), ("WoutT", (DIN, DM)),
        ("W0T", (DM, DFF)), ("W1T", (DFF, DM)), ("b0_r", (128, DFF // 128)),
        ("b1_r", (128, FT)), ("Wb_sl", (DM, 4)), ("Wo_sl", (512, DM)),
        ("Wq_sl", (DM, 512)), ("Wk_sl", (DM, 512)), ("Wv_sl", (DM, 512)),
        ("gmask", (128, 8)), ("ident_f32", (128, 128)),
        ("ones_row_f32", (1, 128)), ("ones_col_f32", (128, 1)),
        ("mask01_mamba", (128, 256)), ("mask_attn", (128, 2048)),
        ("neg_striu2", (128, 128)), ("ident64x2", (128, 128)),
        ("indicator2", (2, 128)), ("swap128", (128, 128)),
        ("misc", (128, 8)), ("ones_full", (128, 256)),
    ]:
        I[name] = di(name, shape)
    ctx.I = I

    ctx.out = nc.dram_tensor("out", [128, FT * TOK], F32, kind="ExternalOutput").ap()

    ctx.x_hbm = nc.dram_tensor("x_hbm", [128, FT * TOK], F32).ap()
    ctx.x_hbm2 = nc.dram_tensor("x_hbm2", [128, FT * TOK], F32).ap()
    ctx.z_hbm = nc.dram_tensor("z_hbm", [128, ZT * TOK], BF16).ap()
    ctx.cbuf_hbm = nc.dram_tensor("cbuf_hbm", [128, TOK], BF16).ap()
    ctx.yn_hbm = nc.dram_tensor("yn_hbm", [128, ZT * TOK], BF16).ap()
    ctx.yp_hbm = nc.dram_tensor("yp_hbm", [128, ZT * TOK], BF16).ap()
    ctx.p_hbm = nc.dram_tensor("p_hbm", [NCK * 128, HP * 128], F32).ap()
    ctx.dec_hbm = nc.dram_tensor("dec_hbm", [NCK * 64, 1], F32).ap()
    ctx.row_hbm = nc.dram_tensor("row_hbm", [8, 1024], F32).ap()
    ctx.cc1_in = nc.dram_tensor("cc1_in", [129, HP * 128], F32)
    ctx.cc1_out = nc.dram_tensor("cc1_out", [4 * 129, HP * 128], F32)
    ctx.cc2_in = nc.dram_tensor("cc2_in", [128, FT * TOK], BF16)
    ctx.cc2_out = nc.dram_tensor("cc2_out", [4 * 128, FT * TOK], BF16)
    ctx.cc3_in = nc.dram_tensor("cc3_in", [144, TOK], F32)
    ctx.cc3_out = nc.dram_tensor("cc3_out", [4 * 144, TOK], F32)
    ctx.o_hbm = nc.dram_tensor("o_hbm", [512, SEQ], BF16).ap()
    ctx.cc5_in = nc.dram_tensor("cc5_in", [4 * DM, TOK], BF16)
    ctx.cc5_out = nc.dram_tensor("cc5_out", [DM, TOK], BF16)

    def mktap(name, shape, dt=F32):
        if name in ctx.tap_requests:
            ctx.taps[name] = (nc.dram_tensor("tap_" + name, list(shape), dt,
                                             kind="ExternalOutput").ap(), shape, dt)
    ctx.mktap = mktap

    build_body(ctx)
    nc.finalize()
    return nc, ctx


def build_body(ctx):
    import contextlib
    nc = ctx.nc
    I = ctx.I
    es = contextlib.ExitStack()
    with es:
        consts = es.enter_context(nc.sbuf_tensor("consts", [128, CONSTW], F32))
        cbf = es.enter_context(nc.sbuf_tensor("cbf", [128, CBFW], BF16))
        psb = [es.enter_context(nc.psum_tensor("psb%d" % i, [128, 512], F32))
               for i in range(6)]
        psbh = [es.enter_context(nc.psum_tensor("psbh%d" % i, [128, 1024], BF16))
                for i in range(2)]
        ctx.consts, ctx.cbf, ctx.psb, ctx.psbh = consts, cbf, psb, psbh

        s = ctx.sem("cload")
        sg = ctx.sem("cloadg")
        sv = ctx.sem("cprep")
        with nc.Block() as blk:
            @blk.sync
            def _(sy):
                for dst, src in [
                    (consts[:, C_IDENT:C_IDENT + 128], I["ident_f32"]),
                    (consts[0:1, C_ONEROW:C_ONEROW + 128], I["ones_row_f32"]),
                    (consts[:, C_MISC:C_MISC + 8], I["misc"]),
                    (consts[0:64, C_DTB:C_DTB + 1], I["dt_bias"]),
                    (consts[:, C_MASKM:C_MASKM + 256], I["mask01_mamba"]),
                    (consts[:, C_DSK:C_DSK + 32], I["Dskip_e"]),
                    (consts[:, C_NRMW:C_NRMW + 32], I["normw_r"]),
                    (consts[:, C_CVB:C_CVB + 34], I["convb_r"]),
                    (consts[:, C_I64:C_I64 + 128], I["ident64x2"]),
                    (consts[:, C_STRIU:C_STRIU + 128], I["neg_striu2"]),
                    (consts[0:2, C_IND2:C_IND2 + 128], I["indicator2"]),
                    (consts[:, C_ONESF:C_ONESF + 256], I["ones_full"]),
                    (consts[0:64, C_ALOG:C_ALOG + 1], I["A_log"]),
                    (consts[:, C_GMASK:C_GMASK + 8], I["gmask"]),
                ]:
                    s.inc(sy.dma_start(out=dst, in_=src), 16)

            @blk.gpsimd
            def _(g):
                sg.inc(g.dma_start(out=cbf[:, CB_ONECOL:CB_ONECOL + 1],
                                   in_=I["ones_col_f32"]), 16)
                sg.inc(g.dma_start(out=cbf[:, CB_IDENT:CB_IDENT + 128],
                                   in_=I["ident_f32"]), 16)
                sg.inc(g.dma_start(out=cbf[0:2, CB_IND2:CB_IND2 + 128],
                                   in_=I["indicator2"]), 16)
                sg.inc(g.dma_start(out=cbf[:, CB_SWAP:CB_SWAP + 128],
                                   in_=I["swap128"]), 16)

            @blk.vector
            def _(v):
                for i in range(6):
                    sv.inc(v.memset(psb[i][:, :], 0.0))

            tgt, tgtg = s.n, sg.n

            @blk.scalar
            def _(sc):
                sc.wait_ge(s.h, tgt)
                sc.wait_ge(sg.h, tgtg)
                seq(sc, sv, sc.activation(out=consts[0:64, C_EXPA:C_EXPA + 1],
                                          in_=consts[0:64, C_ALOG:C_ALOG + 1],
                                          func=AF.Exp))
                sv.inc(sc.activation(out=consts[0:64, C_A:C_A + 1],
                                     in_=consts[0:64, C_EXPA:C_EXPA + 1],
                                     func=AF.Copy, scale=-1.0))
        join(ctx)

        decex = es.enter_context(nc.sbuf_tensor("decex", [128, NCK * 32], F32))
        Dcum = es.enter_context(nc.sbuf_tensor("Dcum", [128, NCK * 32], F32))
        gbf = es.enter_context(nc.sbuf_tensor("gbf", [64, TOK], BF16))
        ctx.decex, ctx.Dcum, ctx.gbf = decex, Dcum, gbf

        zps = contextlib.ExitStack()
        with zps:
            pre = zps.enter_context(nc.sbuf_tensor("pre", [128, XBT * W1027], BF16))
            dtp = zps.enter_context(nc.sbuf_tensor("dtp", [64, TOK], F32))
            ctx.pre, ctx.dtp = pre, dtp
            with nc.sbuf_tensor("xnb1", [128, FT * W1027], BF16) as xnb1:
                emit_ln_stream(ctx, "ln1",
                               lambda f: I["u_fm"][f * 128:(f + 1) * 128, 0:W1027],
                               W1027, xnb1)
                ctx.mktap("xnb1", (128, FT * W1027), BF16)
                if "xnb1" in ctx.taps:
                    ctx.tap_src["xnb1"] = None
                    tap_now(ctx, "xnb1", xnb1[:, :])
                if ctx.upto == "A":
                    finish_taps(ctx)
                    return
                emit_inproj(ctx, xnb1)
                join(ctx)
            emit_convdt(ctx)
            join(ctx)
            ctx.mktap("z", (128, ZT * TOK), BF16)
            if "z" in ctx.taps:
                ctx.tap_src["z"] = ctx.z_hbm
            ctx.mktap("xbc", (128, XBT * W1027), BF16)
            if "xbc" in ctx.taps:
                ctx.tap_src["xbc"] = None
                tap_now(ctx, "xbc", pre[:, :])
            ctx.mktap("dt", (64, TOK))
            if "dt" in ctx.taps:
                ctx.tap_src["dt"] = None
                tap_now(ctx, "dt", dtp[:, :])
            if ctx.upto == "B":
                finish_taps(ctx)
                return
            emit_mamba_p1(ctx)
            if ctx.upto == "C1":
                finish_taps(ctx)
                return
        emit_mamba_p2(ctx)
        if ctx.upto == "C":
            finish_taps(ctx)
            return

        emit_ffn(ctx, ctx.x_hbm, ctx.x_hbm2, "f1")
        ctx.mktap("x2", (128, FT * TOK))
        if "x2" in ctx.taps:
            ctx.tap_src["x2"] = ctx.x_hbm2
        if ctx.upto == "D":
            finish_taps(ctx)
            return

        emit_attnb(ctx, es)
        if ctx.upto in ("E", "E2", "E3", "E4"):
            finish_taps(ctx)
            return

        emit_ffn(ctx, ctx.x_hbm, ctx.out, "f2")
        finish_taps(ctx)


def tap_now(ctx, name, src_ap):
    nc = ctx.nc
    s = ctx.sem("tapnow")
    with nc.Block() as blk:
        @blk.sync
        def _(sy):
            s.inc(sy.dma_start(out=ctx.taps[name][0], in_=src_ap), 16)
            sy.wait_ge(s.h, s.n)


def finish_taps(ctx):
    nc = ctx.nc
    if not ctx.taps:
        return
    s = ctx.sem("tapst")
    with nc.Block() as blk:
        @blk.sync
        def _(sy):
            for name, (ap, shape, dt) in ctx.taps.items():
                src = ctx.tap_src.get(name)
                if src is None:
                    continue
                s.inc(sy.dma_start(out=ap, in_=src), 16)
            sy.wait_ge(s.h, s.n)



def emit_ln(ctx, name, src, W, out):
    nc = ctx.nc
    psb = ctx.psb
    consts = ctx.consts
    cbf = ctx.cbf
    jch = [(0, 512), (512, 512)] + ([(1024, W - 1024)] if W > 1024 else [])
    sA = ctx.sem(name + "a")
    sT = ctx.sem(name + "t")
    sB = ctx.sem(name + "b")
    sC = ctx.sem(name + "c")
    RW = 1040
    with (
        nc.sbuf_tensor(name + "_stat", [128, 4 * RW], BF16) as statb,
        nc.sbuf_tensor(name + "_rows", [1, 3 * RW], F32) as rows,
        nc.sbuf_tensor(name + "_scr", [128, RW], F32) as scr,
        nc.Block() as blk,
    ):
        @blk.vector
        def _(v):
            for f in range(FT):
                sl = (f % 2) * 2 * RW
                if f >= 2:
                    v.wait_ge(sT.h, f - 1)
                seq(v, sA, v.tensor_copy(statb[:, sl:sl + W],
                                         src[:, f * W:(f + 1) * W]))
                ins = v.tensor_mul(statb[:, sl + RW:sl + RW + W],
                                   statb[:, sl:sl + W], statb[:, sl:sl + W])
                sA.inc(ins)

        @blk.tensor
        def _(t):
            for f in range(FT):
                sl = (f % 2) * 2 * RW
                t.wait_ge(sA.h, 2 * (f + 1))
                for ci, (j0, jw) in enumerate(jch):
                    r0 = ci * 32
                    t.matmul(psb[4][r0:r0 + 1, 0:jw], cbf[:, 0:1],
                             statb[:, sl + j0:sl + j0 + jw],
                             start=(f == 0), stop=(f == FT - 1))
                    ins = t.matmul(psb[5][r0:r0 + 1, 0:jw], cbf[:, 0:1],
                                   statb[:, sl + RW + j0:sl + RW + j0 + jw],
                                   start=(f == 0), stop=(f == FT - 1))
                sT.inc(ins)

        @blk.vector
        def _(v):
            v.wait_ge(sT.h, FT)
            for ci, (j0, jw) in enumerate(jch):
                r0 = ci * 32
                v.tensor_scalar(out=rows[0:1, j0:j0 + jw], in0=psb[4][r0:r0 + 1, 0:jw],
                                scalar1=1.0 / DM, scalar2=None, op0=ALU.mult)
                ins = v.tensor_scalar(out=rows[0:1, RW + j0:RW + j0 + jw],
                                      in0=psb[5][r0:r0 + 1, 0:jw],
                                      scalar1=1.0 / DM, scalar2=None, op0=ALU.mult)
            seq(v, sB, ins)
            seq(v, sB, v.tensor_mul(rows[0:1, 2 * RW:2 * RW + W], rows[0:1, 0:W],
                                    rows[0:1, 0:W]))
            ins = v.tensor_sub(rows[0:1, RW:RW + W], rows[0:1, RW:RW + W],
                               rows[0:1, 2 * RW:2 * RW + W])
            sB.inc(ins)

        @blk.scalar
        def _(sc):
            sc.wait_ge(sB.h, 3)
            seq(sc, sB, sc.activation(out=rows[0:1, 2 * RW:2 * RW + W],
                                      in_=rows[0:1, RW:RW + W],
                                      func=AF.Ln, bias=consts[0:1, C_MISC:C_MISC + 1]))
            ins = sc.activation(out=rows[0:1, RW:RW + W],
                                in_=rows[0:1, 2 * RW:2 * RW + W],
                                func=AF.Exp, scale=-0.5)
            sB.inc(ins)

        @blk.tensor
        def _(t):
            t.wait_ge(sB.h, 5)
            for ci, (j0, jw) in enumerate(jch):
                if ci > 0:
                    t.wait_ge(sC.h, 2 * FT * ci)
                t.matmul(psb[6][:, 0:jw], consts[0:1, C_ONEROW:C_ONEROW + 128],
                         rows[0:1, j0:j0 + jw], start=True, stop=True)
                ins = t.matmul(psb[7][:, 0:jw], consts[0:1, C_ONEROW:C_ONEROW + 128],
                               rows[0:1, RW + j0:RW + j0 + jw], start=True, stop=True)
                sB.inc(ins)

        @blk.vector
        def _(v):
            for ci, (j0, jw) in enumerate(jch):
                v.wait_ge(sB.h, 5 + ci + 1)
                for f in range(FT):
                    seq(v, sC, v.tensor_sub(scr[:, 0:jw],
                                            src[:, f * W + j0:f * W + j0 + jw],
                                            psb[6][:, 0:jw]))
                    seq(v, sC, v.tensor_mul(out[:, f * W + j0:f * W + j0 + jw],
                                            scr[:, 0:jw], psb[7][:, 0:jw]))

    join(ctx)



def emit_ln_stream(ctx, name, srcf, W, out):
    nc = ctx.nc
    psb = ctx.psb
    consts = ctx.consts
    cbf = ctx.cbf
    jch = [(0, 512), (512, 512)] + ([(1024, W - 1024)] if W > 1024 else [])
    sld = [ctx.sem("ln_l0"), ctx.sem("ln_l1")]
    sld2 = [ctx.sem("ln_m0"), ctx.sem("ln_m1")]
    sA = ctx.sem("ln_a")
    sT = ctx.sem("ln_t")
    sB = ctx.sem("ln_b")
    sC = ctx.sem("ln_c")
    bld = [sld[0].n, sld[1].n]
    bld2 = [sld2[0].n, sld2[1].n]
    bA, bT, bB, bC = sA.n, sT.n, sB.n, sC.n
    RW = 1040
    mloc = [(2, 0), (3, 0), (2, 509)]
    rloc = [(4, 0), (5, 0), (4, 509)]
    with (
        nc.sbuf_tensor(name + "_xt", [128, 2 * RW], F32) as xt,
        nc.sbuf_tensor(name + "_stat", [128, 2 * RW], BF16) as statb,
        nc.sbuf_tensor(name + "_rows", [1, 3 * RW], F32) as rows,
        nc.sbuf_tensor(name + "_scr", [128, RW], F32) as scr,
        nc.Block() as blk,
    ):
        @blk.sync
        def _(sy):
            for f in range(FT):
                if f >= 2:
                    sy.wait_ge(sA.h, bA + 2 * (f - 1))
                sld[f % 2].inc(sy.dma_start(
                    out=xt[:, (f % 2) * RW:(f % 2) * RW + W],
                    in_=srcf(f)), 16)

        @blk.vector
        def _(v):
            for f in range(FT):
                sl = (f % 2) * RW
                v.wait_ge(sld[f % 2].h, bld[f % 2] + 16 * (f // 2 + 1))
                if f >= 2:
                    v.wait_ge(sT.h, bT + 2 * f - 2)
                seq(v, sA, v.tensor_copy(statb[:, sl:sl + W], xt[:, sl:sl + W]))
                v.wait_ge(sT.h, bT + 2 * f + 1)
                ins = v.tensor_mul(statb[:, sl:sl + W], statb[:, sl:sl + W],
                                   statb[:, sl:sl + W])
                sA.inc(ins)

        @blk.tensor
        def _(t):
            for f in range(FT):
                sl = (f % 2) * RW
                t.wait_ge(sA.h, bA + 2 * f + 1)
                for ci, (j0, jw) in enumerate(jch):
                    ins = t.matmul(psb[4][32 * ci:32 * ci + 1, 0:jw], cbf[:, 0:1],
                                   statb[:, sl + j0:sl + j0 + jw],
                                   start=(f == 0), stop=(f == FT - 1))
                sT.inc(ins)
                t.wait_ge(sA.h, bA + 2 * f + 2)
                for ci, (j0, jw) in enumerate(jch):
                    ins = t.matmul(psb[5][32 * ci:32 * ci + 1, 0:jw], cbf[:, 0:1],
                                   statb[:, sl + j0:sl + j0 + jw],
                                   start=(f == 0), stop=(f == FT - 1))
                sT.inc(ins)

        @blk.vector
        def _(v):
            v.wait_ge(sT.h, bT + 2 * FT)
            for ci, (j0, jw) in enumerate(jch):
                v.tensor_scalar(out=rows[0:1, j0:j0 + jw],
                                in0=psb[4][32 * ci:32 * ci + 1, 0:jw],
                                scalar1=1.0 / DM, scalar2=None, op0=ALU.mult)
                ins = v.tensor_scalar(out=rows[0:1, RW + j0:RW + j0 + jw],
                                      in0=psb[5][32 * ci:32 * ci + 1, 0:jw],
                                      scalar1=1.0 / DM, scalar2=None, op0=ALU.mult)
            seq(v, sB, ins)
            seq(v, sB, v.tensor_mul(rows[0:1, 2 * RW:2 * RW + W], rows[0:1, 0:W],
                                    rows[0:1, 0:W]))
            ins = v.tensor_sub(rows[0:1, RW:RW + W], rows[0:1, RW:RW + W],
                               rows[0:1, 2 * RW:2 * RW + W])
            sB.inc(ins)

        @blk.scalar
        def _(sc):
            sc.wait_ge(sB.h, bB + 3)
            seq(sc, sB, sc.activation(out=rows[0:1, 2 * RW:2 * RW + W],
                                      in_=rows[0:1, RW:RW + W],
                                      func=AF.Ln, bias=consts[0:1, C_MISC:C_MISC + 1]))
            ins = sc.activation(out=rows[0:1, RW:RW + W],
                                in_=rows[0:1, 2 * RW:2 * RW + W],
                                func=AF.Exp, scale=-0.5)
            sB.inc(ins)

        @blk.tensor
        def _(t):
            t.wait_ge(sB.h, bB + 5)
            for ci, (j0, jw) in enumerate(jch):
                (bm, cm) = mloc[ci]
                (br, cr) = rloc[ci]
                t.matmul(psb[bm][:, cm:cm + jw], consts[0:1, C_ONEROW:C_ONEROW + 128],
                         rows[0:1, j0:j0 + jw], start=True, stop=True)
                ins = t.matmul(psb[br][:, cr:cr + jw],
                               consts[0:1, C_ONEROW:C_ONEROW + 128],
                               rows[0:1, RW + j0:RW + j0 + jw], start=True, stop=True)
            sB.inc(ins)

        @blk.sync
        def _(sy):
            for f in range(FT):
                if f == 0:
                    sy.wait_ge(sA.h, bA + 2 * FT)
                if f >= 2:
                    sy.wait_ge(sC.h, bC + 2 * len(jch) * (f - 1))
                sld2[f % 2].inc(sy.dma_start(
                    out=xt[:, (f % 2) * RW:(f % 2) * RW + W],
                    in_=srcf(f)), 16)

        @blk.vector
        def _(v):
            v.wait_ge(sB.h, bB + 6)
            for f in range(FT):
                sl = (f % 2) * RW
                v.wait_ge(sld2[f % 2].h, bld2[f % 2] + 16 * (f // 2 + 1))
                for ci, (j0, jw) in enumerate(jch):
                    (bm, cm) = mloc[ci]
                    (br, cr) = rloc[ci]
                    seq(v, sC, v.tensor_sub(scr[:, 0:jw], xt[:, sl + j0:sl + j0 + jw],
                                            psb[bm][:, cm:cm + jw]))
                    seq(v, sC, v.tensor_mul(out[:, f * W + j0:f * W + j0 + jw],
                                            scr[:, 0:jw], psb[br][:, cr:cr + jw]))
    join(ctx)



def emit_inproj(ctx, xnb1):
    nc = ctx.nc
    I = ctx.I
    psb = ctx.psb
    sw = [ctx.sem("ipw0"), ctx.sem("ipw1")]
    sm = ctx.sem("ipm")
    se = ctx.sem("ipe")
    sf = ctx.sem("ipf")
    STRIP = 3
    NSTR = (OT + STRIP - 1) // STRIP
    tiles_of = lambda st: min(STRIP, OT - st * STRIP)
    strip_cols = lambda st: sum(OTW[st * STRIP + i] for i in range(tiles_of(st)))
    szw = [ctx.sem("ipz0"), ctx.sem("ipz1")]
    with (
        nc.sbuf_tensor("ipwbuf", [128, 2 * FT * STRIP * 128], BF16) as wbuf,
        nc.sbuf_tensor("ipzrot", [128, 2 * 1024], BF16) as zrot,
        nc.Block() as blk,
    ):
        @blk.gpsimd
        def _(g):
            for st in range(NSTR):
                o0 = st * STRIP
                ow = tiles_of(st)
                if st >= 2:
                    g.wait_ge(sf.h, st - 1)
                half = st % 2
                scw = strip_cols(st)
                dst = wbuf[:, half * FT * STRIP * 128:
                           half * FT * STRIP * 128 + FT * scw]
                src = I["WipT"][:, o0 * 128:o0 * 128 + scw]
                sw[half].inc(g.dma_start(
                    out=dst.rearrange("p (k c) -> p k c", k=FT),
                    in_=src.rearrange("(k p) c -> p k c", k=FT)), 16)

        @blk.tensor
        def _(t):
            for o in range(OT):
                st, sub = o // STRIP, o % STRIP
                half = st % 2
                t.wait_ge(sw[half].h, 16 * (st // 2 + 1))
                if o >= 2:
                    t.wait_ge(se.h, o - 1)
                bp = (o % 2) * 2
                is_xbc = 32 <= o <= 65
                scw = strip_cols(st)
                tw = OTW[o]
                for k in range(FT):
                    base = half * FT * STRIP * 128 + k * scw + sub * 128
                    lt = wbuf[:, base: base + tw]
                    rb = xnb1[:, k * W1027:(k + 1) * W1027]
                    t.matmul(psb[bp][0:tw, 0:512], lt, rb[:, 3:515],
                             start=(k == 0), stop=(k == FT - 1))
                    ins = t.matmul(psb[bp + 1][0:tw, 0:512], lt, rb[:, 515:1027],
                                   start=(k == 0), stop=(k == FT - 1))
                    if is_xbc:
                        ins = t.matmul(psb[4][:, (o - 32) * 3:(o - 32) * 3 + 3],
                                       lt, rb[:, 0:3],
                                       start=(k == 0), stop=(k == FT - 1))
                sm.inc(ins)
                if sub == min(STRIP, OT - st * STRIP) - 1:
                    sf.inc(t.drain())

        @blk.scalar
        def _(sc):
            for o in range(OT):
                bp = (o % 2) * 2
                sc.wait_ge(sm.h, o + 1)
                if o < 32:
                    if o >= 2:
                        sc.wait_ge(szw[o % 2].h, 16 * (o // 2))
                    dst = zrot[:, (o % 2) * 1024:(o % 2 + 1) * 1024]
                    sc.activation(out=dst[:, 0:512], in_=psb[bp][:, 0:512], func=AF.Copy)
                    ins = sc.activation(out=dst[:, 512:1024], in_=psb[bp + 1][:, 0:512],
                                        func=AF.Copy)
                elif o <= 65:
                    j = o - 32
                    dst = ctx.pre[:, j * W1027:(j + 1) * W1027]
                    sc.activation(out=dst[:, 3:515], in_=psb[bp][:, 0:512], func=AF.Copy)
                    ins = sc.activation(out=dst[:, 515:1027], in_=psb[bp + 1][:, 0:512],
                                        func=AF.Copy)
                else:
                    sc.activation(out=ctx.dtp[0:64, 0:512], in_=psb[bp][0:64, 0:512],
                                  func=AF.Copy)
                    ins = sc.activation(out=ctx.dtp[0:64, 512:1024],
                                        in_=psb[bp + 1][0:64, 0:512], func=AF.Copy)
                se.inc(ins)
            sc.wait_ge(sm.h, OT)
            for j in range(XBT):
                dst = ctx.pre[:, j * W1027:(j + 1) * W1027]
                ins = sc.activation(out=dst[:, 0:3], in_=psb[4][:, j * 3:j * 3 + 3],
                                    func=AF.Copy)
                se.inc(ins)

        @blk.sync
        def _(sy):
            for o in range(32):
                sy.wait_ge(se.h, o + 1)
                szw[o % 2].inc(sy.dma_start(
                    out=ctx.z_hbm[:, o * 1024:(o + 1) * 1024],
                    in_=zrot[:, (o % 2) * 1024:(o % 2 + 1) * 1024]), 16)
            sy.wait_ge(szw[0].h, szw[0].n)
            sy.wait_ge(szw[1].h, szw[1].n)



def emit_convdt(ctx):
    nc = ctx.nc
    I = ctx.I
    psb = ctx.psb
    consts = ctx.consts
    sw = [ctx.sem("cvw0"), ctx.sem("cvw1")]
    sm = ctx.sem("cvm")
    se = ctx.sem("cve")
    HALF = 17
    with (
        nc.sbuf_tensor("cvd", [128, 2 * 4 * 128], BF16) as cvd,
        nc.Block() as blk,
    ):
        @blk.gpsimd
        def _(g):
            for j in range(XBT):
                if j >= 2:
                    g.wait_ge(sm.h, j - 1)
                sw[j % 2].inc(g.dma_start(
                    out=cvd[:, (j % 2) * 512:(j % 2) * 512 + 512]
                        .rearrange("p (k c) -> p k c", k=4),
                    in_=I["convdiag"][j, :, :, :].rearrange("k p c -> p k c")), 16)

        @blk.tensor
        def _(t):
            for j in range(XBT):
                t.wait_ge(sw[j % 2].h, 16 * (j // 2 + 1))
                bp = (j % 2) * 2
                if j >= 2:
                    t.wait_ge(se.h, j - 1)
                for k in range(4):
                    lt = cvd[:, ((j % 2) * 4 + k) * 128:((j % 2) * 4 + k) * 128 + 128]
                    src = ctx.pre[:, j * W1027:(j + 1) * W1027]
                    t.matmul(psb[bp][:, 0:512], lt, src[:, k:k + 512],
                             start=(k == 0), stop=(k == 3))
                    ins = t.matmul(psb[bp + 1][:, 0:512], lt, src[:, k + 512:k + 1024],
                                   start=(k == 0), stop=(k == 3))
                sm.inc(ins)

        @blk.scalar
        def _(sc):
            for j in range(XBT):
                bp = (j % 2) * 2
                sc.wait_ge(sm.h, j + 1)
                dst = ctx.pre[:, j * W1027:(j + 1) * W1027]
                sc.activation(out=dst[:, 3:515], in_=psb[bp][:, 0:512], func=AF.Silu,
                              bias=consts[:, C_CVB + j:C_CVB + j + 1])
                ins = sc.activation(out=dst[:, 515:1027], in_=psb[bp + 1][:, 0:512],
                                    func=AF.Silu,
                                    bias=consts[:, C_CVB + j:C_CVB + j + 1])
                se.inc(ins)

    join(ctx)
    s = ctx.sem("sdt")
    with (
        nc.sbuf_tensor("dts", [64, TOK], F32) as dts,
        nc.Block() as blk,
    ):
        @blk.scalar
        def _(sc):
            seq(sc, s, sc.activation(out=dts[0:64, :], in_=ctx.dtp[0:64, :],
                                     func=AF.Exp,
                                     bias=consts[0:64, C_DTB:C_DTB + 1]))
            s.inc(sc.activation(out=ctx.dtp[0:64, :], in_=dts[0:64, :], func=AF.Ln,
                                bias=consts[0:64, C_MISC + 2:C_MISC + 3]))
    join(ctx)



def emit_mamba_p1(ctx):
    nc = ctx.nc
    I = ctx.I
    psb = ctx.psb
    consts = ctx.consts
    decex, Dcum, gbf = ctx.decex, ctx.Dcum, ctx.gbf
    import contextlib
    with contextlib.ExitStack() as ms:
        drv = ms.enter_context(nc.sbuf_tensor("drv", [64, 5 * TOK], F32))
        DACS, G, LND, DS, WW = 0, TOK, 2 * TOK, 3 * TOK, 4 * TOK
        colT = ms.enter_context(nc.sbuf_tensor("colT", [128, NCK * 128], F32))
        wT = ms.enter_context(nc.sbuf_tensor("wT", [128, NCK * 128], F32))
        Pb = ms.enter_context(nc.sbuf_tensor("Pb", [128, HP * 128], F32))

        sv = ctx.sem("mbprep")
        sdd = ctx.sem("mbdma")
        stp = ctx.sem("mbtp")
        ste = ctx.sem("mbte")
        with nc.Block() as blk:
            @blk.vector
            def _(v):
                seq(v, sv, v.tensor_scalar(out=drv[:, DS:DS + TOK],
                                           in0=ctx.dtp[0:64, :],
                                           scalar1=consts[0:64, C_A:C_A + 1],
                                           scalar2=None, op0=ALU.mult))
                for c in range(NCK):
                    seq(v, sv, v.tensor_tensor_scan(
                        out=drv[:, DACS + c * MCH:DACS + (c + 1) * MCH],
                        data0=consts[0:64, C_ONESF:C_ONESF + MCH],
                        data1=drv[:, DS + c * MCH:DS + (c + 1) * MCH],
                        initial=0.0, op0=ALU.mult, op1=ALU.add))
                seq(v, sv, v.tensor_scalar(out=drv[:, LND:LND + TOK],
                                           in0=ctx.dtp[0:64, :],
                                           scalar1=consts[0:64, C_MISC + 3:C_MISC + 4],
                                           scalar2=None, op0=ALU.max))

            @blk.scalar
            def _(sc):
                sc.wait_ge(sv.h, sv.n)
                seq(sc, sv, sc.activation(out=drv[:, G:G + TOK],
                                          in_=drv[:, DACS:DACS + TOK], func=AF.Exp))
                seq(sc, sv, sc.activation(out=gbf[0:64, :], in_=drv[:, G:G + TOK],
                                          func=AF.Copy))
                seq(sc, sv, sc.activation(out=drv[:, LND:LND + TOK],
                                          in_=drv[:, LND:LND + TOK], func=AF.Ln))
                for c in range(NCK):
                    seq(sc, sv, sc.activation(
                        out=drv[:, DS + c * MCH:DS + (c + 1) * MCH],
                        in_=drv[:, DACS + c * MCH:DACS + (c + 1) * MCH],
                        func=AF.Exp, scale=-1.0,
                        bias=drv[:, DACS + c * MCH + MCH - 1:DACS + c * MCH + MCH]))

            @blk.vector
            def _(v):
                v.wait_ge(sv.h, sv.n)
                seq(v, sv, v.tensor_mul(drv[:, WW:WW + TOK], ctx.dtp[0:64, :],
                                        drv[:, DS:DS + TOK]))
                seq(v, sv, v.tensor_sub(drv[:, DS:DS + TOK], drv[:, LND:LND + TOK],
                                        drv[:, DACS:DACS + TOK]))
            NPRE = sv.n

            @blk.tensor
            def _(t):
                t.wait_ge(sv.h, NPRE)
                for c in range(NCK):
                    for sblk in range(2):
                        src0 = DS + c * MCH + sblk * 128
                        t.transpose(psb[0][0:128, sblk * 128:sblk * 128 + 64],
                                    drv[:, src0:src0 + 128],
                                    consts[0:64, C_IDENT:C_IDENT + 64])
                        src1 = WW + c * MCH + sblk * 128
                        ins = t.transpose(psb[0][0:128, 256 + sblk * 128:256 + sblk * 128 + 64],
                                          drv[:, src1:src1 + 128],
                                          consts[0:64, C_IDENT:C_IDENT + 64])
                    stp.inc(ins)
                    t.wait_ge(ste.h, c + 1)

            @blk.scalar
            def _(sc):
                for c in range(NCK):
                    sc.wait_ge(stp.h, c + 1)
                    for sblk in range(2):
                        sc.activation(out=colT[:, c * 128 + sblk * 64:c * 128 + sblk * 64 + 64],
                                      in_=psb[0][:, sblk * 128:sblk * 128 + 64],
                                      func=AF.Copy)
                        ins = sc.activation(out=wT[:, c * 128 + sblk * 64:c * 128 + sblk * 64 + 64],
                                            in_=psb[0][:, 256 + sblk * 128:256 + sblk * 128 + 64],
                                            func=AF.Copy)
                    ste.inc(ins)

            @blk.gpsimd
            def _(g):
                g.wait_ge(sv.h, NPRE)
                for c in range(NCK):
                    sdd.inc(g.dma_start(
                        out=ctx.dec_hbm[c * 64:(c + 1) * 64, :],
                        in_=drv[:, G + c * MCH + MCH - 1:G + c * MCH + MCH]), 16)
                g.wait_ge(sdd.h, sdd.n)
                for c in range(NCK):
                    for a in range(2):
                        sdd.inc(g.dma_start(
                            out=decex[a * 64:(a + 1) * 64, c * 32:(c + 1) * 32],
                            in_=bass.AP(ctx.dec_hbm.tensor, c * 64 + a,
                                        [[0, 64], [2, 32]]),
                            allow_slow_non_contiguous=True), 16)

            NDD = sdd.n

            @blk.vector
            def _(v):
                v.wait_ge(sdd.h, NDD)
                seq(v, sv, v.tensor_copy(Dcum[:, 0:32], decex[:, 0:32]))
                for c in range(1, NCK):
                    seq(v, sv, v.tensor_mul(Dcum[:, c * 32:(c + 1) * 32],
                                            Dcum[:, (c - 1) * 32:c * 32],
                                            decex[:, c * 32:(c + 1) * 32]))
                v.memset(Pb[:, :], 0.0)
        join(ctx)

        with (
            nc.sbuf_tensor("btm", [128, 256], BF16) as btm,
            nc.sbuf_tensor("xtb", [128, HP * 256], BF16) as xtb,
            nc.sbuf_tensor("bwb", [128, 2 * 256], BF16) as bwb,
            nc.sbuf_tensor("cbm", [128, 384], BF16) as cbm,
            nc.sbuf_tensor("segb", [128, 2 * 384], F32) as segb,
            nc.sbuf_tensor("ebuf", [128, 2 * 384], BF16) as ebuf,
            nc.sbuf_tensor("mbuf", [128, 2 * 384], BF16) as mbuf,
            nc.sbuf_tensor("ypr", [128, 8 * 256], BF16) as ypr,
        ):
            for c in range(NCK):
                pass1_chunk(ctx, c, drv, colT, wT, decex, Pb, btm, xtb, bwb,
                            cbm, segb, ebuf, mbuf, ypr, gbf)
            join(ctx)

        ctx.mktap("P3", (128, HP * 128))
        if "P3" in ctx.taps:
            ctx.tap_src["P3"] = None
            tap_now(ctx, "P3", Pb[:, :])
        ctx.mktap("G", (64, TOK))
        if "G" in ctx.taps:
            ctx.tap_src["G"] = None
            tap_now(ctx, "G", drv[:, G:G + TOK])
        ctx.mktap("yp", (128, ZT * TOK), BF16)
        if "yp" in ctx.taps:
            ctx.tap_src["yp"] = ctx.yp_hbm

        sxg = ctx.sem("mbxg")
        scc = ctx.sem("mbcc")
        with nc.Block() as blk:
            @blk.sync
            def _(sy):
                s2 = ctx.sem("mbcs")
                s2.inc(sy.dma_start(out=ctx.cbuf_hbm[:, :],
                                    in_=ctx.pre[:, 33 * W1027 + 3:34 * W1027]), 16)
                sy.wait_ge(s2.h, s2.n)

            @blk.gpsimd
            def _(g):
                sxg.inc(g.dma_start(out=ctx.cc1_in.ap()[0:128, :], in_=Pb[:, :]), 16)
                sxg.inc(g.dma_start(
                    out=bass.AP(ctx.cc1_in, 128 * HP * 128, [[32, 128], [1, 32]]),
                    in_=Dcum[:, 3 * 32:4 * 32]), 16)
                g.wait_ge(sxg.h, sxg.n)
                scc.inc(g.collective_compute(
                    "AllGather", ALU.bypass,
                    replica_groups=[[0, 1, 2, 3], [4, 5, 6, 7]],
                    ins=[ctx.cc1_in.ap().opt()],
                    outs=[ctx.cc1_out.ap().opt()]), 1)
        join(ctx)


def emit_mamba_p2(ctx):
    nc = ctx.nc
    I = ctx.I
    psb = ctx.psb
    consts = ctx.consts
    decex, Dcum, gbf = ctx.decex, ctx.Dcum, ctx.gbf
    scc = ctx.sem("mbcc")
    scar = ctx.sem("mbcar")
    import contextlib
    with contextlib.ExitStack() as ms:
        carry = ms.enter_context(nc.sbuf_tensor("carry", [128, HP * 128], F32))
        with (
            nc.sbuf_tensor("peerS", [128, HP * 128], F32) as peerS,
            nc.sbuf_tensor("peerD", [128, 32], F32) as peerD,
            nc.Block() as blk,
        ):
            @blk.vector
            def _(v):
                seq(v, scar, v.memset(carry[:, :], 0.0))

            for j in range(4):
                sj = ctx.sem("mbpx%d" % j)

                @blk.gpsimd
                def _(g, j=j, sj=sj, need=scar.n):
                    g.wait_ge(scc.h, 1)
                    g.wait_ge(scar.h, need)
                    sj.inc(g.dma_start(out=peerS[:, :],
                                       in_=ctx.cc1_out.ap()[j * 129:j * 129 + 128, :]), 16)
                    sj.inc(g.dma_start(
                        out=peerD[:, 0:32],
                        in_=bass.AP(ctx.cc1_out, (j * 129 + 128) * HP * 128,
                                    [[32, 128], [1, 32]])), 16)

                @blk.vector
                def _(v, j=j, sj=sj):
                    v.wait_ge(sj.h, 32)
                    seq(v, scar, v.tensor_scalar(
                        out=peerD[:, 0:32], in0=peerD[:, 0:32],
                        scalar1=consts[:, C_GMASK + j:C_GMASK + j + 1],
                        scalar2=consts[:, C_GMASK1 + j:C_GMASK1 + j + 1],
                        op0=ALU.mult, op1=ALU.add))
                    for hp in range(HP):
                        seq(v, scar, v.tensor_scalar(
                            out=peerS[:, hp * 128:(hp + 1) * 128],
                            in0=peerS[:, hp * 128:(hp + 1) * 128],
                            scalar1=consts[:, C_GMASK + j:C_GMASK + j + 1],
                            scalar2=None, op0=ALU.mult))
                        seq(v, scar, v.scalar_tensor_tensor(
                            out=carry[:, hp * 128:(hp + 1) * 128],
                            in0=carry[:, hp * 128:(hp + 1) * 128],
                            scalar=peerD[:, hp:hp + 1],
                            in1=peerS[:, hp * 128:(hp + 1) * 128],
                            op0=ALU.mult, op1=ALU.add))
        join(ctx)

        with (
            nc.sbuf_tensor("prevb", [128, HP * 128], F32) as prevb,
            nc.sbuf_tensor("Ptmp", [128, HP * 128], F32) as Ptmp,
            nc.sbuf_tensor("pvt", [128, HP * 128], BF16) as pvt,
            nc.sbuf_tensor("ypt", [128, HP * 256], BF16) as ypt,
            nc.sbuf_tensor("zst", [128, ZT * 256], BF16) as zst,
            nc.sbuf_tensor("cbuf", [128, 256], BF16) as cbuf2,
            nc.sbuf_tensor("gbc", [128, 2 * 256], BF16) as gbc,
            nc.sbuf_tensor("silb", [128, 2 * 256], BF16) as silb,
            nc.sbuf_tensor("ygb", [128, ZT * 256], BF16) as ygb,
            nc.sbuf_tensor("sqb", [128, 2 * 256], BF16) as sqb,
            nc.sbuf_tensor("rrow", [1, 256], F32) as rrow,
            nc.sbuf_tensor("ynb", [128, ZT * 256], BF16) as ynb,
        ):
            for c in range(NCK):
                pass2_chunk(ctx, c, carry, prevb, Ptmp, pvt, ypt, zst, cbuf2,
                            gbc, silb, ygb, sqb, rrow, ynb)
            join(ctx)

    emit_outproj(ctx)
    join(ctx)
    ctx.mktap("x1", (128, FT * TOK))
    if "x1" in ctx.taps:
        ctx.tap_src["x1"] = ctx.x_hbm


def pass2_chunk(ctx, c, carry, prevb, Ptmp, pvt, ypt, zst, cbuf2, gbc, silb,
                ygb, sqb, rrow, ynb):
    nc = ctx.nc
    I = ctx.I
    psb = ctx.psb
    consts = ctx.consts
    decex, Dcum, gbf = ctx.decex, ctx.Dcum, ctx.gbf
    tc = 3 + c * MCH
    sld = [ctx.sem("p2ld0"), ctx.sem("p2ld1")]
    spv = ctx.sem("p2pv")
    stp = ctx.sem("p2tp")
    ste = ctx.sem("p2te")
    syo = ctx.sem("p2yo")
    sgb = ctx.sem("p2gb")
    sya = ctx.sem("p2ya")
    ssl = ctx.sem("p2sl")
    sss = ctx.sem("p2ss")
    srn = ctx.sem("p2rn")
    syn = ctx.sem("p2yn")
    syw = ctx.sem("p2yw")
    par = c % 2
    b_sld = [sld[0].n, sld[1].n]
    b_spv, b_stp, b_ste, b_syo = spv.n, stp.n, ste.n, syo.n
    b_sgb, b_sya, b_ssl, b_sss, b_srn, b_syn, b_syw = (
        sgb.n, sya.n, ssl.n, sss.n, srn.n, syn.n, syw.n)

    with nc.Block() as blk:
        @blk.sync
        def _(sy):
            if c >= 1:
                sld[par].inc(sy.dma_start(out=Ptmp[:, :],
                                          in_=ctx.p_hbm[(c - 1) * 128:c * 128, :]), 16)
            sld[par].inc(sy.dma_start(
                out=ypt[:, :],
                in_=ctx.yp_hbm[:, c * HP * MCH:(c + 1) * HP * MCH]), 16)
            sld[par].inc(sy.dma_start(
                out=zst[:, :],
                in_=bass.AP(ctx.z_hbm.tensor, c * MCH,
                            [[ZT * TOK, 128], [TOK, ZT], [1, MCH]])), 16)
            sld[par].inc(sy.dma_start(
                out=cbuf2[:, :],
                in_=ctx.cbuf_hbm[:, c * MCH:(c + 1) * MCH]), 16)

        @blk.vector
        def _(v):
            v.wait_ge(sld[par].h, sld[par].n)
            for hp in range(HP):
                if c == 0:
                    seq(v, spv, v.tensor_copy(prevb[:, hp * 128:(hp + 1) * 128],
                                              carry[:, hp * 128:(hp + 1) * 128]))
                else:
                    seq(v, spv, v.scalar_tensor_tensor(
                        out=prevb[:, hp * 128:(hp + 1) * 128],
                        in0=carry[:, hp * 128:(hp + 1) * 128],
                        scalar=Dcum[:, (c - 1) * 32 + hp:(c - 1) * 32 + hp + 1],
                        in1=Ptmp[:, hp * 128:(hp + 1) * 128],
                        op0=ALU.mult, op1=ALU.add))

        @blk.tensor
        def _(t):
            for grp in range(8):
                t.wait_ge(spv.h, b_spv + (grp + 1) * 4)
                if grp >= 1:
                    t.wait_ge(ste.h, b_ste + grp)
                for u in range(4):
                    hp = grp * 4 + u
                    ins = t.transpose(psb[0][:, u * 128:(u + 1) * 128],
                                      prevb[:, hp * 128:(hp + 1) * 128],
                                      consts[:, C_IDENT:C_IDENT + 128])
                stp.inc(ins)

        @blk.scalar
        def _(sc):
            for grp in range(8):
                sc.wait_ge(stp.h, b_stp + grp + 1)
                ins = sc.activation(out=pvt[:, grp * 512:(grp + 1) * 512],
                                    in_=psb[0][:, 0:512], func=AF.Copy)
                ste.inc(ins)

        @blk.tensor
        def _(t):
            t.wait_ge(sld[par].h, sld[par].n)
            for hp in range(HP):
                bo = 1 + (hp % 2)
                bg = 3 + (hp % 2)
                t.wait_ge(ste.h, b_ste + hp // 4 + 1)
                if hp >= 2:
                    t.wait_ge(sya.h, b_sya + 4 * (hp - 2) + 1)
                    t.wait_ge(sgb.h, b_sgb + hp - 1)
                for q in range(2):
                    t.matmul(psb[bo][64 * q:64 * q + 64, 0:256],
                             pvt[:, hp * 128 + 64 * q:hp * 128 + 64 * q + 64],
                             cbuf2[:, 0:256], start=True, stop=True)
                for q in range(2):
                    rs = bass.AP(ctx.cbf, CB_IDENT + 2 * hp + q,
                                 [[CBFW, 64], [0, 64]])
                    ins = t.matmul(psb[bg][64 * q:64 * q + 64, 0:256], rs,
                                   gbf[0:64, c * MCH:(c + 1) * MCH],
                                   start=True, stop=True)
                syo.inc(ins)
                if hp >= 2:
                    hq = hp - 2
                    t.wait_ge(sya.h, b_sya + 4 * (hq + 1))
                    ins = t.matmul(psb[5][0:1, 0:256], ctx.cbf[:, 0:1],
                                   sqb[:, (hq % 2) * 256:(hq % 2) * 256 + 256],
                                   start=(hq == 0), stop=False)
                    sss.inc(ins)
            for hq in (HP - 2, HP - 1):
                t.wait_ge(sya.h, b_sya + 4 * (hq + 1))
                ins = t.matmul(psb[5][0:1, 0:256], ctx.cbf[:, 0:1],
                               sqb[:, (hq % 2) * 256:(hq % 2) * 256 + 256],
                               start=False, stop=(hq == HP - 1))
                sss.inc(ins)

        @blk.scalar
        def _(sc):
            for hp in range(HP):
                bg = 3 + (hp % 2)
                sc.wait_ge(syo.h, b_syo + hp + 1)
                if hp >= 2:
                    sc.wait_ge(sya.h, b_sya + 4 * (hp - 2) + 3)
                ins = sc.activation(out=gbc[:, (hp % 2) * 256:(hp % 2) * 256 + 256],
                                    in_=psb[bg][:, 0:256], func=AF.Copy)
                sgb.inc(ins)
                ins = sc.activation(out=silb[:, (hp % 2) * 256:(hp % 2) * 256 + 256],
                                    in_=zst[:, hp * 256:(hp + 1) * 256], func=AF.Silu)
                ssl.inc(ins)

        @blk.vector
        def _(v):
            for hp in range(HP):
                bo = 1 + (hp % 2)
                sl = (hp % 2) * 256
                v.wait_ge(sgb.h, b_sgb + hp + 1)
                v.wait_ge(ssl.h, b_ssl + hp + 1)
                seq(v, sya, v.tensor_mul(ygb[:, hp * 256:(hp + 1) * 256],
                                         psb[bo][:, 0:256], gbc[:, sl:sl + 256]))
                seq(v, sya, v.tensor_add(ygb[:, hp * 256:(hp + 1) * 256],
                                         ygb[:, hp * 256:(hp + 1) * 256],
                                         ypt[:, hp * 256:(hp + 1) * 256]))
                seq(v, sya, v.tensor_mul(ygb[:, hp * 256:(hp + 1) * 256],
                                         ygb[:, hp * 256:(hp + 1) * 256],
                                         silb[:, sl:sl + 256]))
                if hp >= 2:
                    v.wait_ge(sss.h, b_sss + hp - 1)
                seq(v, sya, v.tensor_mul(sqb[:, sl:sl + 256],
                                         ygb[:, hp * 256:(hp + 1) * 256],
                                         ygb[:, hp * 256:(hp + 1) * 256]))

        @blk.vector
        def _(v):
            v.wait_ge(sss.h, b_sss + HP)
            seq(v, srn, v.tensor_scalar(out=rrow[0:1, :], in0=psb[5][0:1, 0:256],
                                        scalar1=1.0 / DIN, scalar2=None,
                                        op0=ALU.mult))

        @blk.scalar
        def _(sc):
            sc.wait_ge(srn.h, b_srn + 1)
            seq(sc, srn, sc.activation(out=rrow[0:1, :], in_=rrow[0:1, :],
                                       func=AF.Ln,
                                       bias=consts[0:1, C_MISC + 4:C_MISC + 5]))
            seq(sc, srn, sc.activation(out=rrow[0:1, :], in_=rrow[0:1, :],
                                       func=AF.Exp, scale=-0.5))

        @blk.tensor
        def _(t):
            t.wait_ge(srn.h, b_srn + 3)
            ins = t.matmul(psb[0][:, 0:256], consts[0:1, C_ONEROW:C_ONEROW + 128],
                           rrow[0:1, 0:256], start=True, stop=True)
            srn.inc(ins)

        @blk.vector
        def _(v):
            v.wait_ge(srn.h, b_srn + 4)
            for hp in range(HP):
                seq(v, syn, v.scalar_tensor_tensor(
                    out=ynb[:, hp * 256:(hp + 1) * 256],
                    in0=ygb[:, hp * 256:(hp + 1) * 256],
                    scalar=consts[:, C_NRMW + hp:C_NRMW + hp + 1],
                    in1=psb[0][:, 0:256], op0=ALU.mult, op1=ALU.mult))

        @blk.sync
        def _(sy):
            sy.wait_ge(syn.h, b_syn + HP)
            syw.inc(sy.dma_start(
                out=bass.AP(ctx.yn_hbm.tensor, c * MCH,
                            [[ZT * TOK, 128], [TOK, ZT], [1, MCH]]),
                in_=ynb[:, :]), 16)
            sy.wait_ge(syw.h, syw.n)
    join(ctx)


def emit_outproj(ctx):
    nc = ctx.nc
    I = ctx.I
    psb = ctx.psb
    sw = [ctx.sem("opw0"), ctx.sem("opw1")]
    syl = [ctx.sem("opy0"), ctx.sem("opy1")]
    sxl = [ctx.sem("opx0"), ctx.sem("opx1")]
    sm = ctx.sem("opm")
    se = ctx.sem("ope")
    sxw = [ctx.sem("opxw0"), ctx.sem("opxw1")]
    OSTR = 2
    NSTR = FT // OSTR
    with (
        nc.sbuf_tensor("opwbuf", [128, 2 * ZT * OSTR * 128], BF16) as wbuf,
        nc.sbuf_tensor("opyn", [128, ZT * 1024], BF16) as ynl,
        nc.sbuf_tensor("opxt", [128, 2 * TOK], F32) as xtl,
        nc.sbuf_tensor("opxo", [128, 2 * TOK], F32) as xo,
        nc.Block() as blk,
    ):
        @blk.sync
        def _(sy):
            syl[0].inc(sy.dma_start(out=ynl[:, :], in_=ctx.yn_hbm[:, :]), 16)
            for o in range(2):
                sxl[o % 2].inc(sy.dma_start(
                    out=xtl[:, (o % 2) * TOK:(o % 2 + 1) * TOK],
                    in_=I["u_fm"][o * 128:(o + 1) * 128, 3:W1027]), 16)
            for o in range(FT):
                sy.wait_ge(se.h, 2 * (o + 1))
                sxw[o % 2].inc(sy.dma_start(
                    out=ctx.x_hbm[:, o * TOK:(o + 1) * TOK],
                    in_=xo[:, (o % 2) * TOK:(o % 2 + 1) * TOK]), 16)
                if o + 2 < FT:
                    sxl[o % 2].inc(sy.dma_start(
                        out=xtl[:, (o % 2) * TOK:(o % 2 + 1) * TOK],
                        in_=I["u_fm"][(o + 2) * 128:(o + 3) * 128, 3:W1027]), 16)
            sy.wait_ge(sxw[0].h, sxw[0].n)
            sy.wait_ge(sxw[1].h, sxw[1].n)

        @blk.gpsimd
        def _(g):
            for st in range(NSTR):
                if st >= 2:
                    g.wait_ge(sm.h, OSTR * (st - 1))
                half = st % 2
                scw = OSTR * 128
                dst = wbuf[:, half * ZT * OSTR * 128:
                           half * ZT * OSTR * 128 + ZT * scw]
                src = I["WoutT"][:, st * scw:(st + 1) * scw]
                sw[half].inc(g.dma_start(
                    out=dst.rearrange("p (k c) -> p k c", k=ZT),
                    in_=src.rearrange("(k p) c -> p k c", k=ZT)), 16)

        @blk.tensor
        def _(t):
            t.wait_ge(syl[0].h, 16)
            for o in range(FT):
                st, sub = o // OSTR, o % OSTR
                half = st % 2
                t.wait_ge(sw[half].h, 16 * (st // 2 + 1))
                if o >= 2:
                    t.wait_ge(se.h, 2 * (o - 1))
                bp = (o % 2) * 2
                scw = OSTR * 128
                for k in range(ZT):
                    base = half * ZT * OSTR * 128 + k * scw + sub * 128
                    lt = wbuf[:, base:base + 128]
                    t.matmul(psb[bp][:, 0:512], lt, ynl[:, k * TOK:k * TOK + 512],
                             start=(k == 0), stop=(k == ZT - 1))
                    ins = t.matmul(psb[bp + 1][:, 0:512], lt,
                                   ynl[:, k * TOK + 512:(k + 1) * TOK],
                                   start=(k == 0), stop=(k == ZT - 1))
                sm.inc(ins)

        @blk.vector
        def _(v):
            for o in range(FT):
                bp = (o % 2) * 2
                v.wait_ge(sm.h, o + 1)
                v.wait_ge(sxl[o % 2].h, 16 * (o // 2 + 1))
                if o >= 2:
                    v.wait_ge(sxw[o % 2].h, 16 * (o // 2))
                seq(v, se, v.tensor_add(xo[:, (o % 2) * TOK:(o % 2) * TOK + 512],
                                        psb[bp][:, 0:512],
                                        xtl[:, (o % 2) * TOK:(o % 2) * TOK + 512]))
                ins = v.tensor_add(xo[:, (o % 2) * TOK + 512:(o % 2 + 1) * TOK],
                                   psb[bp + 1][:, 0:512],
                                   xtl[:, (o % 2) * TOK + 512:(o % 2 + 1) * TOK])
                se.inc(ins)


def pass1_chunk(ctx, c, drv, colT, wT, decex, Pb, btm, xtb, bwb,
                cbm, segb, ebuf, mbuf, ypr, gbf):
    nc = ctx.nc
    psb = ctx.psb
    consts = ctx.consts
    DACS = 0
    tc = 3 + c * MCH
    pre = ctx.pre
    stp = ctx.sem("p1tp")
    ste = ctx.sem("p1te")
    scb = ctx.sem("p1cb")
    scv = ctx.sem("p1cv")
    sbc = ctx.sem("p1bc")
    ssg = ctx.sem("p1sg")
    sex = ctx.sem("p1ex")
    smm = ctx.sem("p1m")
    sbw = ctx.sem("p1bw")
    syd = ctx.sem("p1yd")
    sev = ctx.sem("p1ev")
    syw = [ctx.sem("p1yw0"), ctx.sem("p1yw1")]
    spw = ctx.sem("p1pw")
    b_stp, b_ste, b_scb, b_scv, b_sbc, b_ssg, b_sex = (
        stp.n, ste.n, scb.n, scv.n, sbc.n, ssg.n, sex.n)
    b_smm, b_sbw, b_syd, b_sev = smm.n, sbw.n, syd.n, sev.n
    b_syw = [syw[0].n, syw[1].n]

    xre = pre
    B_TILE, C_TILE = 32, 33

    with nc.Block() as blk:
        @blk.tensor
        def _(t):
            t.wait_ge(sev.h, b_sev)
            t.wait_ge(ste.h, b_ste)
            for sblk in range(2):
                ins = t.transpose(ctx.psbh[0][:, sblk * 128:(sblk + 1) * 128],
                                  pre[:, B_TILE * W1027 + tc + sblk * 128:
                                      B_TILE * W1027 + tc + (sblk + 1) * 128],
                                  ctx.cbf[:, CB_IDENT:CB_IDENT + 128])
            stp.inc(ins)
            for grp in range(16):
                t.wait_ge(ste.h, b_ste + (grp + 1 if grp > 0 else 0))
                for u in range(4):
                    idx = grp * 4 + u
                    hp, sblk = idx // 2, idx % 2
                    ins = t.transpose(
                        ctx.psbh[1][:, u * 128:(u + 1) * 128],
                        xre[:, hp * W1027 + tc + sblk * 128:
                            hp * W1027 + tc + (sblk + 1) * 128],
                        ctx.cbf[:, CB_IDENT:CB_IDENT + 128])
                stp.inc(ins)

        @blk.scalar
        def _(sc):
            sc.wait_ge(syd.h, b_syd)
            sc.wait_ge(sbw.h, b_sbw)
            sc.wait_ge(stp.h, b_stp + 1)
            ins = sc.activation(out=btm[:, 0:256], in_=ctx.psbh[0][:, 0:256],
                                func=AF.Copy)
            ste.inc(ins)
            for grp in range(16):
                sc.wait_ge(stp.h, b_stp + 2 + grp)
                ins = sc.activation(out=xtb[:, grp * 512:(grp + 1) * 512],
                                    in_=ctx.psbh[1][:, 0:512], func=AF.Copy)
                ste.inc(ins)

        @blk.tensor
        def _(t):
            t.wait_ge(scv.h, b_scv)
            t.wait_ge(smm.h, b_smm)
            for sblk in range(2):
                ins = t.matmul(psb[0][:, sblk * 256:(sblk + 1) * 256],
                               pre[:, B_TILE * W1027 + tc + sblk * 128:
                                   B_TILE * W1027 + tc + (sblk + 1) * 128],
                               pre[:, C_TILE * W1027 + tc:C_TILE * W1027 + tc + 256],
                               start=True, stop=True)
                scb.inc(ins)

        @blk.vector
        def _(v):
            v.wait_ge(scb.h, b_scb + 2)
            seq(v, scv, v.tensor_mul(cbm[:, 0:256], psb[0][:, 0:256],
                                     consts[:, C_MASKM:C_MASKM + 256]))
            seq(v, scv, v.tensor_mul(cbm[:, 256:384], psb[0][:, 256 + 128:512],
                                     consts[:, C_MASKM:C_MASKM + 128]))

        @blk.tensor
        def _(t):
            t.wait_ge(ste.h, b_ste + 17)
            t.wait_ge(scv.h, b_scv + 2)
            for h in range(NH):
                hp, q = h // 2, h % 2
                if h >= 2:
                    t.wait_ge(ssg.h, b_ssg + 2 * (h - 1))
                bb = 1 + (h % 2)
                rowsel = bass.AP(consts, C_IDENT + h, [[CONSTW, 64], [0, 128]])
                ins = t.matmul(psb[bb][:, 0:256], rowsel,
                               drv[0:64, DACS + c * MCH:DACS + (c + 1) * MCH],
                               start=True, stop=True)
                sbc.inc(ins)
                if h >= 1:
                    emit_head_mms(ctx, t, c, h - 1, xtb, bwb, mbuf, psb, syd, smm,
                                  sbw, sev, b_smm, b_sbw, b_sev)
            emit_head_mms(ctx, t, c, NH - 1, xtb, bwb, mbuf, psb, syd, smm,
                          sbw, sev, b_smm, b_sbw, b_sev)

        @blk.vector
        def _(v):
            for h in range(NH):
                hp, q = h // 2, h % 2
                par = h % 2
                if h >= 2:
                    v.wait_ge(syd.h, b_syd + h - 1)
                v.wait_ge(ste.h, b_ste + 1)
                seq(v, sbw, v.tensor_scalar(
                    out=bwb[:, par * 256:par * 256 + 128], in0=btm[:, 0:128],
                    scalar1=wT[:, c * 128 + 0 * 64 + h:c * 128 + 0 * 64 + h + 1],
                    scalar2=None, op0=ALU.mult))
                seq(v, sbw, v.tensor_scalar(
                    out=bwb[:, par * 256 + 128:par * 256 + 256], in0=btm[:, 128:256],
                    scalar1=wT[:, c * 128 + 1 * 64 + h:c * 128 + 1 * 64 + h + 1],
                    scalar2=None, op0=ALU.mult))
                v.wait_ge(sbc.h, b_sbc + h + 1)
                seq(v, ssg, v.tensor_scalar(
                    out=segb[:, par * 384:par * 384 + 256],
                    in0=psb[1 + (h % 2)][:, 0:256],
                    scalar1=colT[:, c * 128 + 0 * 64 + h:c * 128 + 0 * 64 + h + 1],
                    scalar2=40.0, op0=ALU.add, op1=ALU.min))
                seq(v, ssg, v.tensor_scalar(
                    out=segb[:, par * 384 + 256:par * 384 + 384],
                    in0=psb[1 + (h % 2)][:, 128:256],
                    scalar1=colT[:, c * 128 + 1 * 64 + h:c * 128 + 1 * 64 + h + 1],
                    scalar2=40.0, op0=ALU.add, op1=ALU.min))
                v.wait_ge(sex.h, b_sex + 2 * (h + 1))
                seq(v, smm, v.tensor_mul(mbuf[:, par * 384:par * 384 + 256],
                                         ebuf[:, par * 384:par * 384 + 256],
                                         cbm[:, 0:256]))
                seq(v, smm, v.tensor_mul(mbuf[:, par * 384 + 256:par * 384 + 384],
                                         ebuf[:, par * 384 + 256:par * 384 + 384],
                                         cbm[:, 256:384]))
                if h >= 3 and h % 2 == 1:
                    emit_hp_evac(ctx, v, c, (h - 3) // 2, decex, Pb, ypr, psb,
                                 syd, sev, syw, b_syd, b_sev, b_syw)
            emit_hp_evac(ctx, v, c, HP - 1, decex, Pb, ypr, psb, syd, sev, syw,
                         b_syd, b_sev, b_syw)

        @blk.scalar
        def _(sc):
            for h in range(NH):
                par = h % 2
                sc.wait_ge(ssg.h, b_ssg + 2 * (h + 1))
                sex.inc(sc.activation(out=ebuf[:, par * 384:par * 384 + 256],
                                      in_=segb[:, par * 384:par * 384 + 256],
                                      func=AF.Exp))
                sex.inc(sc.activation(out=ebuf[:, par * 384 + 256:par * 384 + 384],
                                      in_=segb[:, par * 384 + 256:par * 384 + 384],
                                      func=AF.Exp))

        @blk.sync
        def _(sy):
            for grp in range(8):
                sy.wait_ge(sev.h, b_sev + 2 * (grp * 4 + 4))
                syw[grp % 2].inc(sy.dma_start(
                    out=ctx.yp_hbm[:, (c * HP + grp * 4) * MCH:
                                   (c * HP + (grp + 1) * 4) * MCH],
                    in_=ypr[:, (grp % 2) * 1024:(grp % 2 + 1) * 1024]), 16)
            sy.wait_ge(sev.h, b_sev + 2 * HP)
            spw.inc(sy.dma_start(out=ctx.p_hbm[c * 128:(c + 1) * 128, :],
                                 in_=Pb[:, :]), 16)
            sy.wait_ge(syw[0].h, syw[0].n)
            sy.wait_ge(syw[1].h, syw[1].n)
            sy.wait_ge(spw.h, spw.n)
    join(ctx)


def emit_head_mms(ctx, t, c, h, xtb, bwb, mbuf, psb, syd, smm, sbw, sev,
                  b_smm, b_sbw, b_sev):
    hp, q = h // 2, h % 2
    par = h % 2
    by = 3 + (hp % 2)
    bs = 5 if hp % 2 == 0 else 0
    t.wait_ge(smm.h, b_smm + 2 * (h + 1))
    t.wait_ge(sbw.h, b_sbw + 2 * (h + 1))
    if hp >= 2 and q == 0:
        t.wait_ge(sev.h, b_sev + 2 * (hp - 1))
    xbase = hp * 256
    t.matmul(psb[by][64 * q:64 * q + 64, 0:256],
             xtb[:, xbase + 64 * q:xbase + 64 * q + 64],
             mbuf[:, par * 384:par * 384 + 256], start=True, stop=False)
    t.matmul(psb[by][64 * q:64 * q + 64, 128:256],
             xtb[:, xbase + 128 + 64 * q:xbase + 128 + 64 * q + 64],
             mbuf[:, par * 384 + 256:par * 384 + 384], start=False, stop=True)
    t.matmul(psb[bs][64 * q:64 * q + 64, 0:128],
             xtb[:, xbase + 64 * q:xbase + 64 * q + 64],
             bwb[:, par * 256:par * 256 + 128], start=True, stop=False)
    ins = t.matmul(psb[bs][64 * q:64 * q + 64, 0:128],
                   xtb[:, xbase + 128 + 64 * q:xbase + 128 + 64 * q + 64],
                   bwb[:, par * 256 + 128:par * 256 + 256], start=False, stop=True)
    syd.inc(ins)


def emit_hp_evac(ctx, v, c, hp, decex, Pb, ypr, psb, syd, sev, syw,
                 b_syd, b_sev, b_syw):
    consts = ctx.consts
    tc = 3 + c * MCH
    by = 3 + (hp % 2)
    bs = 5 if hp % 2 == 0 else 0
    v.wait_ge(syd.h, b_syd + 2 * hp + 2)
    grp = hp // 4
    if grp >= 2:
        v.wait_ge(syw[grp % 2].h, b_syw[grp % 2] + 16 * (grp // 2))
    slot = (grp % 2) * 1024 + (hp % 4) * 256
    seq(v, sev, v.scalar_tensor_tensor(
        out=ypr[:, slot:slot + 256],
        in0=ctx.pre[:, hp * W1027 + tc:hp * W1027 + tc + 256],
        scalar=consts[:, C_DSK + hp:C_DSK + hp + 1],
        in1=psb[by][:, 0:256], op0=ALU.mult, op1=ALU.add))
    seq(v, sev, v.scalar_tensor_tensor(
        out=Pb[:, hp * 128:(hp + 1) * 128],
        in0=Pb[:, hp * 128:(hp + 1) * 128],
        scalar=decex[:, c * 32 + hp:c * 32 + hp + 1],
        in1=psb[bs][:, 0:128], op0=ALU.mult, op1=ALU.add))


def atb_tform(ctx, p, Btb, Bbb):
    nc = ctx.nc
    psb = ctx.psb
    psbh = ctx.psbh
    consts = ctx.consts
    cbf = ctx.cbf
    st = ctx.sem("p1te")
    sv = ctx.sem("p1cv")
    knfm, btT, Zbuf = ctx.knfm, ctx.btT, ctx.Zbuf
    zoff = p * 4096

    bt, bv = st.n, sv.n
    with nc.Block() as blk:
        @blk.tensor
        def _(t):
            for w in range(8):
                if w >= 2:
                    t.wait_ge(sv.h, bv + 3 * (w - 2) + 1)
                for j in range(4 * w, 4 * w + 4):
                    sl = knfm[:, p * SEQ + j * 128:p * SEQ + (j + 1) * 128]
                    ins = t.matmul(psb[w % 2][:, (j % 4) * 128:(j % 4) * 128 + 128],
                                   sl, sl, start=True, stop=True)
                st.inc(ins)
                t.wait_ge(sv.h, bv + 3 * w + 1)
                if w >= 2:
                    t.wait_ge(sv.h, bv + 3 * (w - 2) + 3)
                for j in range(4 * w, 4 * w + 4):
                    ins = t.transpose(psbh[w % 2][:, (j % 4) * 128:(j % 4) * 128 + 128],
                                      Btb[:, j * 128:(j + 1) * 128],
                                      cbf[:, CB_IDENT:CB_IDENT + 128])
                st.inc(ins)

        @blk.vector
        def _(v):
            for w in range(8):
                v.wait_ge(st.h, bt + 2 * w + 1)
                for j in range(4 * w, 4 * w + 4):
                    ins = v.scalar_tensor_tensor(
                        out=Btb[:, j * 128:(j + 1) * 128],
                        in0=psb[w % 2][:, (j % 4) * 128:(j % 4) * 128 + 128],
                        scalar=btT[:, j * 4 + p:j * 4 + p + 1],
                        in1=consts[:, C_STRIU:C_STRIU + 128],
                        op0=ALU.mult, op1=ALU.mult)
                sv.inc(ins)
                for j in range(4 * w, 4 * w + 4):
                    v.wait_ge(sv.h, sv.n)
                    ins = v.tensor_add(Zbuf[:, zoff + j * 128:zoff + (j + 1) * 128],
                                       Btb[:, j * 128:(j + 1) * 128],
                                       consts[:, C_I64:C_I64 + 128])
                sv.inc(ins)
                v.wait_ge(st.h, bt + 2 * w + 2)
                sv.inc(v.tensor_copy(Bbb[:, w * 512:(w + 1) * 512],
                                     psbh[w % 2][:, 0:512]))

    for k in range(1, 6):
        bt, bv = st.n, sv.n
        with nc.Block() as blk:
            @blk.tensor
            def _(t):
                for w in range(8):
                    bC = psb[2 + w % 2]
                    bD = psb[4 + w % 2]
                    bZ = psb[0 + w % 2]
                    if w >= 2:
                        t.wait_ge(sv.h, bv + 3 * (w - 2) + 1)
                    for j in range(4 * w, 4 * w + 4):
                        c = (j % 4) * 128
                        ins = t.matmul(bC[:, c:c + 128],
                                       Btb[:, j * 128:(j + 1) * 128],
                                       Bbb[:, j * 128:(j + 1) * 128],
                                       start=True, stop=True)
                    st.inc(ins)
                    if w >= 2:
                        t.wait_ge(sv.h, bv + 3 * (w - 2) + 2)
                    for j in range(4 * w, 4 * w + 4):
                        c = (j % 4) * 128
                        ins = t.matmul(bD[:, c:c + 128],
                                       Bbb[:, j * 128:(j + 1) * 128],
                                       Btb[:, j * 128:(j + 1) * 128],
                                       start=True, stop=True)
                    st.inc(ins)
                    t.wait_ge(sv.h, bv + 3 * w + 1)
                    if w >= 2:
                        t.wait_ge(sv.h, bv + 3 * (w - 2) + 3)
                    for j in range(4 * w, 4 * w + 4):
                        c = (j % 4) * 128
                        ins = t.matmul(bZ[:, c:c + 128],
                                       Bbb[:, j * 128:(j + 1) * 128],
                                       Zbuf[:, zoff + j * 128:zoff + (j + 1) * 128],
                                       start=True, stop=True)
                    st.inc(ins)

            @blk.vector
            def _(v):
                for w in range(8):
                    bC = psb[2 + w % 2]
                    bD = psb[4 + w % 2]
                    bZ = psb[0 + w % 2]
                    v.wait_ge(st.h, bt + 3 * w + 2)
                    sv.inc(v.tensor_copy(Bbb[:, w * 512:(w + 1) * 512], bC[:, 0:512]))
                    sv.inc(v.tensor_copy(Btb[:, w * 512:(w + 1) * 512], bD[:, 0:512]))
                    v.wait_ge(st.h, bt + 3 * w + 3)
                    v.wait_ge(sv.h, sv.n)
                    sv.inc(v.tensor_add(Zbuf[:, zoff + w * 512:zoff + (w + 1) * 512],
                                        Zbuf[:, zoff + w * 512:zoff + (w + 1) * 512],
                                        bZ[:, 0:512]))


def atb_scan(ctx):
    import os
    nc = ctx.nc
    psb = ctx.psb
    psbh = ctx.psbh
    cbf = ctx.cbf
    st = ctx.sem("p2te")
    sv = ctx.sem("p2pv")
    ss = ctx.sem("p2rn")
    knfm, vfm, btT = ctx.knfm, ctx.vfm, ctx.btT
    Zbuf, knTs = ctx.Zbuf, ctx.knTs
    r2b, rhsb, vnbb, vnt = ctx.r2b, ctx.rhsb, ctx.vnbb, ctx.vnt
    Sacc, Sb = ctx.Sacc, ctx.Sb
    NSC = int(os.environ.get("ATB_SCAN_STEPS", str(ANC)))
    FH0 = os.environ.get("ATB_FORCEHALF0", "") == "1"
    bt, bv, bs = st.n, sv.n, ss.n
    with nc.Block() as blk:
        @blk.tensor
        def _(t):
            for i in range(NSC):
                half = 0 if FH0 else i % 2
                j = i // 2
                r0 = half * 64
                slot = i % 2
                if i > 0:
                    t.wait_ge(sv.h, bv + 6 * i)
                    for p in range(NPAIR):
                        ins = t.matmul(psb[0][:, p * 64:(p + 1) * 64],
                                       Sb[:, p * 128:(p + 1) * 128],
                                       knfm[:, p * SEQ + i * 64:p * SEQ + i * 64 + 64],
                                       start=True, stop=True)
                    st.inc(ins)
                t.wait_ge(sv.h, bv + 6 * i + 1)
                for p in range(NPAIR):
                    ins = t.transpose(psbh[0][r0:r0 + 64, p * 128:(p + 1) * 128],
                                      rhsb[:, slot * 256 + p * 64:slot * 256 + (p + 1) * 64],
                                      cbf[:, CB_IDENT:CB_IDENT + 128])
                st.inc(ins)
                if i > 0:
                    t.wait_ge(sv.h, bv + 6 * i - 3)
                for p in range(NPAIR):
                    ins = t.transpose(psbh[1][r0:r0 + 64, p * 128:(p + 1) * 128],
                                      knfm[:, p * SEQ + i * 64:p * SEQ + i * 64 + 64],
                                      cbf[:, CB_IDENT:CB_IDENT + 128])
                st.inc(ins)
                t.wait_ge(sv.h, bv + 6 * i + 2)
                if i >= 2:
                    t.wait_ge(ss.h, bs + i - 1)
                pV = psb[1] if FH0 else psb[1 + 2 * (i % 2)]
                for p in range(NPAIR):
                    ins = t.matmul(pV[r0:r0 + 64, p * 128:(p + 1) * 128],
                                   Zbuf[r0:r0 + 64,
                                        p * 4096 + j * 128 + r0:p * 4096 + j * 128 + r0 + 64],
                                   r2b[r0:r0 + 64, p * 128:(p + 1) * 128],
                                   start=True, stop=True)
                st.inc(ins)
                t.wait_ge(ss.h, bs + i + 1)
                t.wait_ge(sv.h, bv + 6 * i + 3)
                for p in range(NPAIR):
                    ins = t.matmul(psb[5][:, p * 128:(p + 1) * 128],
                                   knTs[r0:r0 + 64, p * 128:(p + 1) * 128],
                                   vnbb[r0:r0 + 64, p * 128:(p + 1) * 128],
                                   start=True, stop=True)
                st.inc(ins)

        @blk.vector
        def _(v):
            for i in range(NSC):
                half = 0 if FH0 else i % 2
                j = i // 2
                r0 = half * 64
                slot = i % 2
                vap = bass.AP(vfm, i * 64, [[NPAIR * SEQ, 128], [SEQ, 4], [1, 64]])
                if i == 0:
                    sv.inc(v.tensor_copy(rhsb[:, 0:256], vap))
                else:
                    v.wait_ge(st.h, bt + 5 * i)
                    sv.inc(v.tensor_sub(rhsb[:, slot * 256:slot * 256 + 256],
                                        vap, psb[0][:, 0:256]))
                v.wait_ge(st.h, bt + 5 * i + 1)
                sv.inc(v.tensor_copy(r2b[r0:r0 + 64, 0:512], psbh[0][r0:r0 + 64, 0:512]))
                v.wait_ge(st.h, bt + 5 * i + 2)
                sv.inc(v.tensor_copy(knTs[r0:r0 + 64, 0:512], psbh[1][r0:r0 + 64, 0:512]))
                v.wait_ge(st.h, bt + 5 * i + 3)
                pV = psb[1] if FH0 else psb[1 + 2 * (i % 2)]
                sv.inc(v.tensor_copy(vnt[r0:r0 + 64, j * 512:(j + 1) * 512],
                                     pV[r0:r0 + 64, 0:512]))
                v.wait_ge(st.h, bt + 5 * i + 4)
                if i == 0:
                    sv.inc(v.tensor_copy(Sacc[:, 0:512], psb[5][:, 0:512]))
                else:
                    v.wait_ge(sv.h, bv + 6 * i)
                    sv.inc(v.tensor_add(Sacc[:, 0:512], Sacc[:, 0:512],
                                        psb[5][:, 0:512]))
                v.wait_ge(sv.h, sv.n)
                sv.inc(v.tensor_copy(Sb[:, 0:512], Sacc[:, 0:512]))

        @blk.scalar
        def _(sc):
            for i in range(NSC):
                half = 0 if FH0 else i % 2
                j = i // 2
                r0 = half * 64
                sc.wait_ge(st.h, bt + 5 * i + 3)
                sc.wait_ge(sv.h, bv + 6 * i + 4)
                pV = psb[1] if FH0 else psb[1 + 2 * (i % 2)]
                for p in range(NPAIR):
                    ins = sc.activation(out=vnbb[r0:r0 + 64, p * 128:(p + 1) * 128],
                                        in_=pV[r0:r0 + 64, p * 128:(p + 1) * 128],
                                        func=AF.Copy,
                                        scale=btT[r0:r0 + 64, j * 4 + p:j * 4 + p + 1])
                ss.inc(ins)


def atb_attn(ctx):
    import contextlib
    nc = ctx.nc
    psb = ctx.psb
    consts = ctx.consts
    cbf = ctx.cbf
    st = ctx.sem("opm")
    ss = ctx.sem("ope")
    sv = ctx.sem("p2ya")
    opy = [ctx.sem("opy0"), ctx.sem("opy1")]
    scc = ctx.sem("mbcc")
    qfm, knfm, vnt = ctx.qfm, ctx.knfm, ctx.vnt
    btTs, maskb = ctx.btTs, ctx.maskb
    groups = [(p, qb) for p in range(NPAIR) for qb in range(8)]
    Ks = [4 * qb + 4 for (p, qb) in groups]
    bt0, bs0, bv0 = st.n, ss.n, sv.n
    bts = [bt0]
    bss = [bs0]
    bvs = [bv0]
    for K in Ks:
        bts.append(bts[-1] + 2 * K + 2)
        bss.append(bss[-1] + K + 2)
        bvs.append(bvs[-1] + K + 6)
    bopy = None
    with contextlib.ExitStack() as xs_:
        expS = xs_.enter_context(nc.sbuf_tensor("expS", [128, 32 * 512], BF16))
        acc = xs_.enter_context(nc.sbuf_tensor("acc", [128, 512], F32))
        scrO = xs_.enter_context(nc.sbuf_tensor("scrO", [128, 512], F32))
        obuf = xs_.enter_context(nc.sbuf_tensor("obuf", [128, 2 * 512], BF16))
        rwb = xs_.enter_context(nc.sbuf_tensor("rwb", [1, 2 * 1024], F32))
        bopy = [opy[0].n, opy[1].n]

        def opy_after(g):
            return (g % 2, bopy[g % 2] + 16 * (g // 2 + 1))
        with nc.Block() as blk:
            @blk.tensor
            def _(t):
                for g, (p, qb) in enumerate(groups):
                    K = Ks[g]
                    bt, bs, bv = bts[g], bss[g], bvs[g]
                    for kt in range(K):
                        if kt >= 2:
                            t.wait_ge(ss.h, bs + kt - 1)
                        st.inc(t.matmul(psb[kt % 2][:, 0:512],
                                        knfm[:, p * SEQ + kt * 128:p * SEQ + (kt + 1) * 128],
                                        qfm[:, p * SEQ + qb * 512:p * SEQ + (qb + 1) * 512],
                                        start=True, stop=True))
                    t.wait_ge(sv.h, bv + K + 4)
                    t.wait_ge(ss.h, bs)
                    st.inc(t.matmul(psb[4][32:33, 0:512],
                                    consts[:, C_MISC + 2:C_MISC + 3],
                                    acc[:, 0:512], start=True, stop=True))
                    pav = psb[2 + g % 2]
                    for kt in range(K):
                        if kt == 0 and g >= 2:
                            t.wait_ge(sv.h, bvs[g - 2] + Ks[g - 2] + 5)
                        if kt >= 4 * qb:
                            mc = min(kt - 4 * qb + 1, 4)
                            t.wait_ge(sv.h, bv + kt + mc)
                        else:
                            t.wait_ge(ss.h, bs + kt + 1)
                        st.inc(t.matmul(pav[:, 0:512],
                                        vnt[:, kt * 512 + p * 128:kt * 512 + (p + 1) * 128],
                                        expS[:, kt * 512:(kt + 1) * 512],
                                        start=(kt == 0), stop=(kt == K - 1)))
                    t.wait_ge(ss.h, bs + K + 2)
                    t.wait_ge(sv.h, bv)
                    st.inc(t.matmul(psb[5][:, 0:512],
                                    consts[0:1, C_ONEROW:C_ONEROW + 128],
                                    rwb[0:1, (g % 2) * 1024 + 512:(g % 2) * 1024 + 1024],
                                    start=True, stop=True))

            @blk.scalar
            def _(sc):
                for g, (p, qb) in enumerate(groups):
                    K = Ks[g]
                    bt, bs, bv = bts[g], bss[g], bvs[g]
                    for kt in range(K):
                        sc.wait_ge(st.h, bt + kt + 1)
                        ss.inc(sc.activation(out=expS[:, kt * 512:(kt + 1) * 512],
                                             in_=psb[kt % 2][:, 0:512], func=AF.Exp,
                                             scale=btTs[:, kt * 4 + p:kt * 4 + p + 1]))
                    sc.wait_ge(st.h, bt + K + 1)
                    ss.inc(sc.activation(out=rwb[0:1, (g % 2) * 1024:(g % 2) * 1024 + 512],
                                         in_=psb[4][32:33, 0:512], func=AF.Ln))
                    sc.wait_ge(ss.h, ss.n)
                    ss.inc(sc.activation(out=rwb[0:1, (g % 2) * 1024 + 512:(g % 2) * 1024 + 1024],
                                         in_=rwb[0:1, (g % 2) * 1024:(g % 2) * 1024 + 512],
                                         func=AF.Exp, scale=-1.0))

            @blk.vector
            def _(v):
                for g, (p, qb) in enumerate(groups):
                    K = Ks[g]
                    bt, bs, bv = bts[g], bss[g], bvs[g]
                    for kt in range(K):
                        if kt >= 4 * qb:
                            v.wait_ge(ss.h, bs + kt + 1)
                            sv.inc(v.tensor_mul(expS[:, kt * 512:(kt + 1) * 512],
                                                expS[:, kt * 512:(kt + 1) * 512],
                                                maskb[:, (kt - 4 * qb) * 512:(kt - 4 * qb + 1) * 512]))
                            v.wait_ge(sv.h, sv.n)
                        else:
                            v.wait_ge(ss.h, bs + kt + 1)
                        if kt == 0:
                            sv.inc(v.tensor_copy(acc[:, 0:512], expS[:, 0:512]))
                        else:
                            v.wait_ge(sv.h, sv.n)
                            sv.inc(v.tensor_add(acc[:, 0:512], acc[:, 0:512],
                                                expS[:, kt * 512:(kt + 1) * 512]))
                    v.wait_ge(st.h, bt + 2 * K + 1)
                    sv.inc(v.tensor_copy(scrO[:, 0:512], psb[2 + g % 2][:, 0:512]))
                    v.wait_ge(st.h, bt + 2 * K + 2)
                    v.wait_ge(sv.h, sv.n)
                    if g >= 2:
                        sl, cnt = opy_after(g - 2)
                        v.wait_ge(opy[sl].h, cnt)
                    sv.inc(v.tensor_mul(obuf[:, (g % 2) * 512:(g % 2 + 1) * 512],
                                        scrO[:, 0:512], psb[5][:, 0:512]))

            @blk.sync
            def _(sy):
                for g, (p, qb) in enumerate(groups):
                    K = Ks[g]
                    sy.wait_ge(sv.h, bvs[g] + K + 6)
                    opy[g % 2].inc(sy.dma_start(
                        out=ctx.o_hbm[p * 128:(p + 1) * 128, qb * 512:(qb + 1) * 512],
                        in_=obuf[:, (g % 2) * 512:(g % 2 + 1) * 512]), 16)
                sy.wait_ge(opy[0].h, opy[0].n)
                sy.wait_ge(opy[1].h, opy[1].n)



def atb_wo(ctx):
    import contextlib
    nc = ctx.nc
    I = ctx.I
    psb = ctx.psb
    scc = ctx.sem("mbcc")
    scg = ctx.sem("cloadg")
    sol = [ctx.sem("p2ld0"), ctx.sem("p2ld1")]
    syw = [ctx.sem("opy0"), ctx.sem("opy1")]
    sm = ctx.sem("ff_m1")
    se = ctx.sem("ff_e1")
    with contextlib.ExitStack() as ws:
        wos = ws.enter_context(nc.sbuf_tensor("wos", [128, 4 * DM], BF16))
        osb = ws.enter_context(nc.sbuf_tensor("osb", [128, 2 * 4 * 512], BF16))
        ysb = ws.enter_context(nc.sbuf_tensor("ysb", [128, 2 * 512], BF16))
        bm, be = sm.n, se.n
        bol = [sol[0].n, sol[1].n]
        byw = [syw[0].n, syw[1].n]
        ywtot = [byw[0] + 16 * 64, byw[1] + 16 * 64]
        brs = scc.n + 1
        bscg = scg.n
        with nc.Block() as blk:
            @blk.sync
            def _(sy):
                def load(tc):
                    if tc >= 2:
                        sy.wait_ge(sm.h, bm + 16 * (tc - 1))
                    for kk in range(4):
                        sol[tc % 2].inc(sy.dma_start(
                            out=osb[:, (tc % 2) * 2048 + kk * 512:
                                    (tc % 2) * 2048 + (kk + 1) * 512],
                            in_=ctx.o_hbm[kk * 128:(kk + 1) * 128,
                                          tc * 512:(tc + 1) * 512]), 16)

                def write(i):
                    tc, o = divmod(i, 16)
                    sy.wait_ge(se.h, be + i + 1)
                    r0 = (tc // 2) * DM + o * 128
                    c0 = (tc % 2) * 512
                    syw[i % 2].inc(sy.dma_start(
                        out=ctx.cc5_in.ap()[r0:r0 + 128, c0:c0 + 512],
                        in_=ysb[:, (i % 2) * 512:(i % 2 + 1) * 512]), 16)
                load(0)
                load(1)
                for tc in range(2, 8):
                    for o in range(16):
                        write(16 * (tc - 2) + o)
                    load(tc)
                for i in range(96, 128):
                    write(i)
                sy.wait_ge(syw[0].h, syw[0].n)
                sy.wait_ge(syw[1].h, syw[1].n)

            @blk.tensor
            def _(t):
                t.wait_ge(scg.h, bscg + 16)
                for tc in range(8):
                    t.wait_ge(sol[tc % 2].h, bol[tc % 2] + 64 * (tc // 2 + 1))
                    for o in range(16):
                        i = tc * 16 + o
                        bank = psb[2 + i % 2]
                        if i >= 2:
                            t.wait_ge(se.h, be + i - 1)
                        for kk in range(4):
                            ins = t.matmul(
                                bank[:, 0:512],
                                wos[:, kk * DM + o * 128:kk * DM + (o + 1) * 128],
                                osb[:, (tc % 2) * 2048 + kk * 512:
                                    (tc % 2) * 2048 + (kk + 1) * 512],
                                start=(kk == 0), stop=(kk == 3))
                        sm.inc(ins)

            @blk.vector
            def _(v):
                for i in range(8 * 16):
                    bank = psb[2 + i % 2]
                    v.wait_ge(sm.h, bm + i + 1)
                    if i >= 2:
                        v.wait_ge(syw[i % 2].h, byw[i % 2] + 16 * (i // 2))
                    se.inc(v.tensor_copy(ysb[:, (i % 2) * 512:(i % 2 + 1) * 512],
                                         bank[:, 0:512]))

            @blk.gpsimd
            def _(g):
                scg.inc(g.dma_start(
                    out=wos[:, :].rearrange("p (k c) -> p k c", k=4),
                    in_=I["Wo_sl"].rearrange("(k p) c -> p k c", k=4)), 16)
                g.wait_ge(scg.h, scg.n)
                g.wait_ge(syw[0].h, ywtot[0])
                g.wait_ge(syw[1].h, ywtot[1])
                scc.inc(g.collective_compute(
                    "ReduceScatter", ALU.add,
                    replica_groups=[[0, 1, 2, 3], [4, 5, 6, 7]],
                    ins=[ctx.cc5_in.ap().opt()],
                    outs=[ctx.cc5_out.ap().opt()]), 1)

    with contextlib.ExitStack() as ws:
        ctb = ws.enter_context(nc.sbuf_tensor("ctb", [128, 2 * TOK], BF16))
        xts = ws.enter_context(nc.sbuf_tensor("wxts", [128, 2 * TOK], F32))
        xov = ws.enter_context(nc.sbuf_tensor("wxov", [128, 2 * TOK], F32))
        sxl = [ctx.sem("ff_xl0"), ctx.sem("ff_xl1")]
        sxw = [ctx.sem("ff_xw0"), ctx.sem("ff_xw1")]
        sv = ctx.sem("ff_e0")
        bxl = [sxl[0].n, sxl[1].n]
        bxw = [sxw[0].n, sxw[1].n]
        bv = sv.n
        with nc.Block() as blk:
            @blk.sync
            def _(sy):
                sy.wait_ge(scc.h, brs)

                def load(o):
                    s = o % 2
                    sxl[s].inc(sy.dma_start(
                        out=ctb[:, s * TOK:(s + 1) * TOK],
                        in_=ctx.cc5_out.ap()[o * 128:(o + 1) * 128, :]), 16)
                    sxl[s].inc(sy.dma_start(
                        out=xts[:, s * TOK:(s + 1) * TOK],
                        in_=ctx.x_hbm2[:, o * TOK:(o + 1) * TOK]), 16)
                load(0)
                load(1)
                for o in range(FT):
                    sy.wait_ge(sv.h, bv + o + 1)
                    sxw[o % 2].inc(sy.dma_start(
                        out=ctx.x_hbm[:, o * TOK:(o + 1) * TOK],
                        in_=xov[:, (o % 2) * TOK:(o % 2 + 1) * TOK]), 16)
                    if o + 2 < FT:
                        load(o + 2)
                sy.wait_ge(sxw[0].h, sxw[0].n)
                sy.wait_ge(sxw[1].h, sxw[1].n)

            @blk.vector
            def _(v):
                for o in range(FT):
                    v.wait_ge(sxl[o % 2].h, bxl[o % 2] + 32 * (o // 2 + 1))
                    if o >= 2:
                        v.wait_ge(sxw[o % 2].h, bxw[o % 2] + 16 * (o // 2))
                    ins = v.tensor_add(xov[:, (o % 2) * TOK:(o % 2 + 1) * TOK],
                                       xts[:, (o % 2) * TOK:(o % 2 + 1) * TOK],
                                       ctb[:, (o % 2) * TOK:(o % 2 + 1) * TOK])
                    sv.inc(ins)


def emit_ffn(ctx, src_hbm, dst_hbm, tag):
    import contextlib
    nc = ctx.nc
    I = ctx.I
    psb = ctx.psb
    with contextlib.ExitStack() as fs:
        fb = fs.enter_context(nc.sbuf_tensor("ffb" + tag, [128, 80], F32))
        sfb = ctx.sem("ff_fb")
        with nc.Block() as blk:
            @blk.sync
            def _(sy):
                sfb.inc(sy.dma_start(out=fb[:, 0:64], in_=I["b0_r"]), 16)
                sfb.inc(sy.dma_start(out=fb[:, 64:80], in_=I["b1_r"]), 16)
                sy.wait_ge(sfb.h, sfb.n)
        xnb = fs.enter_context(nc.sbuf_tensor("ffx" + tag, [128, FT * TOK], BF16))
        emit_ln_stream(ctx, "ln" + tag,
                       lambda f: src_hbm[:, f * TOK:(f + 1) * TOK], TOK, xnb)
        hbuf = fs.enter_context(nc.sbuf_tensor("ffh" + tag, [128, 64 * 512], BF16))
        w0b = fs.enter_context(nc.sbuf_tensor("ffw0" + tag, [128, 2 * FT * 256], BF16))
        w1b = fs.enter_context(nc.sbuf_tensor("ffw1" + tag, [128, 2 * 64 * 128], BF16))
        xts = fs.enter_context(nc.sbuf_tensor("ffxt" + tag, [128, 2 * 512], F32))
        xov = fs.enter_context(nc.sbuf_tensor("ffxo" + tag, [128, 2 * 512], F32))
        for half in range(2):
            t0 = half * 512
            sw0 = [ctx.sem("ff_w0a"), ctx.sem("ff_w0b")]
            sw1 = [ctx.sem("ff_w1a"), ctx.sem("ff_w1b")]
            sxl = [ctx.sem("ff_xl0"), ctx.sem("ff_xl1")]
            sm0 = ctx.sem("ff_m0")
            se0 = ctx.sem("ff_e0")
            sm1 = ctx.sem("ff_m1")
            se1 = ctx.sem("ff_e1")
            sxw = [ctx.sem("ff_xw0"), ctx.sem("ff_xw1")]
            bw0 = [sw0[0].n, sw0[1].n]
            bw1 = [sw1[0].n, sw1[1].n]
            bxl = [sxl[0].n, sxl[1].n]
            bxw = [sxw[0].n, sxw[1].n]
            bm0, be0, bm1, be1 = sm0.n, se0.n, sm1.n, se1.n
            with nc.Block() as blk:
                @blk.gpsimd
                def _(g):
                    for st in range(32):
                        if st >= 2:
                            g.wait_ge(sm0.h, bm0 + 2 * (st - 1))
                        hf = st % 2
                        dst = w0b[:, hf * FT * 256:hf * FT * 256 + FT * 256]
                        srcw = I["W0T"][:, st * 256:(st + 1) * 256]
                        sw0[hf].inc(g.dma_start(
                            out=dst.rearrange("p (k c) -> p k c", k=FT),
                            in_=srcw.rearrange("(k p) c -> p k c", k=FT)), 16)
                    for st in range(FT):
                        if st >= 2:
                            g.wait_ge(sm1.h, bm1 + st - 1)
                        hf = st % 2
                        dst = w1b[:, hf * 64 * 128:hf * 64 * 128 + 64 * 128]
                        srcw = I["W1T"][:, st * 128:(st + 1) * 128]
                        sw1[hf].inc(g.dma_start(
                            out=dst.rearrange("p (k c) -> p k c", k=64),
                            in_=srcw.rearrange("(k p) c -> p k c", k=64)), 16)

                @blk.tensor
                def _(t):
                    for o in range(64):
                        st, sub = o // 2, o % 2
                        hf = st % 2
                        t.wait_ge(sw0[hf].h, bw0[hf] + 16 * (st // 2 + 1))
                        if o >= 2:
                            t.wait_ge(se0.h, be0 + o - 1)
                        bp = o % 2
                        base = hf * FT * 256 + sub * 128
                        for k in range(FT):
                            ins = t.matmul(psb[bp][:, 0:512],
                                           w0b[:, base + k * 256:base + k * 256 + 128],
                                           xnb[:, k * TOK + t0:k * TOK + t0 + 512],
                                           start=(k == 0), stop=(k == FT - 1))
                        sm0.inc(ins)
                    for d in range(FT):
                        hf = d % 2
                        t.wait_ge(sw1[hf].h, bw1[hf] + 16 * (d // 2 + 1))
                        t.wait_ge(se0.h, be0 + 64)
                        if d >= 2:
                            t.wait_ge(se1.h, be1 + d - 1)
                        bp = 2 + d % 2
                        for k in range(64):
                            base = hf * 64 * 128 + k * 128
                            ins = t.matmul(psb[bp][:, 0:512],
                                           w1b[:, base:base + 128],
                                           hbuf[:, k * 512:(k + 1) * 512],
                                           start=(k == 0), stop=(k == 63))
                        sm1.inc(ins)

                @blk.scalar
                def _(sc):
                    for o in range(64):
                        bp = o % 2
                        sc.wait_ge(sm0.h, bm0 + o + 1)
                        ins = sc.activation(out=hbuf[:, o * 512:(o + 1) * 512],
                                            in_=psb[bp][:, 0:512], func=AF.Gelu,
                                            bias=fb[:, o:o + 1])
                        se0.inc(ins)

                @blk.sync
                def _(sy):
                    for d in range(2):
                        sxl[d % 2].inc(sy.dma_start(
                            out=xts[:, (d % 2) * 512:(d % 2 + 1) * 512],
                            in_=src_hbm[:, d * TOK + t0:d * TOK + t0 + 512]), 16)
                    for d in range(FT):
                        sy.wait_ge(se1.h, be1 + d + 1)
                        sxw[d % 2].inc(sy.dma_start(
                            out=dst_hbm[:, d * TOK + t0:d * TOK + t0 + 512],
                            in_=xov[:, (d % 2) * 512:(d % 2 + 1) * 512]), 16)
                        if d + 2 < FT:
                            sxl[d % 2].inc(sy.dma_start(
                                out=xts[:, (d % 2) * 512:(d % 2 + 1) * 512],
                                in_=src_hbm[:, (d + 2) * TOK + t0:(d + 2) * TOK + t0 + 512]), 16)
                    sy.wait_ge(sxw[0].h, sxw[0].n)
                    sy.wait_ge(sxw[1].h, sxw[1].n)

                @blk.vector
                def _(v):
                    for d in range(FT):
                        bp = 2 + d % 2
                        v.wait_ge(sm1.h, bm1 + d + 1)
                        v.wait_ge(sxl[d % 2].h, bxl[d % 2] + 16 * (d // 2 + 1))
                        if d >= 2:
                            v.wait_ge(sxw[d % 2].h, bxw[d % 2] + 16 * (d // 2))
                        ins = v.scalar_tensor_tensor(
                            out=xov[:, (d % 2) * 512:(d % 2 + 1) * 512],
                            in0=psb[bp][:, 0:512],
                            scalar=fb[:, 64 + d:65 + d],
                            in1=xts[:, (d % 2) * 512:(d % 2 + 1) * 512],
                            op0=ALU.add, op1=ALU.add)
                        se1.inc(ins)
            join(ctx)



def emit_attnb(ctx, es):
    import contextlib
    nc = ctx.nc
    I = ctx.I
    with contextlib.ExitStack() as ats:
        with nc.sbuf_tensor("xnb3", [128, FT * TOK], BF16) as xnb3:
            emit_ln_stream(ctx, "ln3",
                           lambda f: ctx.x_hbm2[:, f * TOK:(f + 1) * TOK],
                           TOK, xnb3)
            sW = ctx.sem("ipz0")
            scc = ctx.sem("mbcc")
            bw = sW.n
            with nc.Block() as blk:
                @blk.sync
                def _(sy):
                    sW.inc(sy.dma_start(out=ctx.cc2_in.ap(), in_=xnb3[:, :]), 16)
                    sy.wait_ge(sW.h, sW.n)

                @blk.gpsimd
                def _(g):
                    g.wait_ge(sW.h, bw + 16)
                    scc.inc(g.collective_compute(
                        "AllGather", ALU.bypass,
                        replica_groups=[[0, 1, 2, 3], [4, 5, 6, 7]],
                        ins=[ctx.cc2_in.ap().opt()],
                        outs=[ctx.cc2_out.ap().opt()]), 1)
            ctx.cc2_ready = scc.n

        qfm = ats.enter_context(nc.sbuf_tensor("qfm", [128, NPAIR * SEQ], BF16))
        knfm = ats.enter_context(nc.sbuf_tensor("knfm", [128, NPAIR * SEQ], BF16))
        vfm = ats.enter_context(nc.sbuf_tensor("vfm", [128, NPAIR * SEQ], BF16))
        btT = ats.enter_context(nc.sbuf_tensor("btT", [128, 128], F32))
        btTs = ats.enter_context(nc.sbuf_tensor("btTs", [128, 128], F32))
        maskb = ats.enter_context(nc.sbuf_tensor("maskb", [128, 2048], BF16))
        ctx.qfm, ctx.knfm, ctx.vfm = qfm, knfm, vfm
        ctx.btT, ctx.btTs, ctx.maskb = btT, btTs, maskb

        atb_proj(ctx)
        ctx.mktap("beta", (128, 128))
        if "beta" in ctx.taps:
            ctx.tap_src["beta"] = None
            tap_now(ctx, "beta", btT[:, :])
        ctx.mktap("knfm", (128, NPAIR * SEQ), BF16)
        if "knfm" in ctx.taps:
            ctx.tap_src["knfm"] = None
            tap_now(ctx, "knfm", knfm[:, :])
        ctx.mktap("qfm", (128, NPAIR * SEQ), BF16)
        if "qfm" in ctx.taps:
            ctx.tap_src["qfm"] = None
            tap_now(ctx, "qfm", qfm[:, :])
        ctx.mktap("vfm", (128, NPAIR * SEQ), BF16)
        if "vfm" in ctx.taps:
            ctx.tap_src["vfm"] = None
            tap_now(ctx, "vfm", vfm[:, :])
        if ctx.upto == "E2":
            finish_taps(ctx)
            return

        vnt = ats.enter_context(nc.sbuf_tensor("vnt", [128, 32 * 512], BF16))
        ctx.vnt = vnt
        with contextlib.ExitStack() as scs:
            Zbuf = scs.enter_context(
                nc.sbuf_tensor("zbuf", [128, NPAIR * 4096], BF16))
            Sb = scs.enter_context(nc.sbuf_tensor("sbS", [128, 512], BF16))
            knTs = scs.enter_context(nc.sbuf_tensor("knts", [128, 512], BF16))
            r2b = scs.enter_context(nc.sbuf_tensor("r2b", [128, 512], BF16))
            rhsb = scs.enter_context(nc.sbuf_tensor("rhsb", [128, 2 * 256], BF16))
            vnbb = scs.enter_context(nc.sbuf_tensor("vnbb", [128, 512], BF16))
            Sacc = scs.enter_context(nc.sbuf_tensor("sacc", [128, 512], F32))
            ctx.Zbuf, ctx.knTs, ctx.Sb = Zbuf, knTs, Sb
            ctx.r2b, ctx.rhsb, ctx.vnbb, ctx.Sacc = r2b, rhsb, vnbb, Sacc
            with (
                nc.sbuf_tensor("btb", [128, 32 * 128], BF16) as Btb,
                nc.sbuf_tensor("bbb", [128, 32 * 128], BF16) as Bbb,
            ):
                for p in range(NPAIR):
                    atb_tform(ctx, p, Btb, Bbb)
            ctx.mktap("zb", (128, NPAIR * 4096), BF16)
            if "zb" in ctx.taps:
                ctx.tap_src["zb"] = None
                tap_now(ctx, "zb", Zbuf[:, :])
            if ctx.upto == "E3":
                finish_taps(ctx)
                return
            atb_scan(ctx)
        ctx.mktap("vnt", (128, 32 * 512), BF16)
        if "vnt" in ctx.taps:
            ctx.tap_src["vnt"] = None
            tap_now(ctx, "vnt", vnt[:, :])
        if ctx.upto == "E4":
            finish_taps(ctx)
            return
        atb_attn(ctx)
        ctx.mktap("ofm", (512, SEQ), BF16)
        if "ofm" in ctx.taps:
            ctx.tap_src["ofm"] = ctx.o_hbm
    atb_wo(ctx)
    ctx.mktap("x3", (128, FT * TOK))
    if "x3" in ctx.taps:
        ctx.tap_src["x3"] = ctx.x_hbm


def atb_proj(ctx):
    import contextlib
    nc = ctx.nc
    I = ctx.I
    psb = ctx.psb
    consts = ctx.consts
    cbf = ctx.cbf
    scg = ctx.sem("cloadg")
    opx = [ctx.sem("opx0"), ctx.sem("opx1")]
    tld = [ctx.sem("p2ld0"), ctx.sem("p2ld1")]
    scc = ctx.sem("mbcc")
    st = ctx.sem("p1m")
    sv = ctx.sem("p1ev")
    ss = ctx.sem("p1ex")
    with contextlib.ExitStack() as ps:
        wqb = ps.enter_context(nc.sbuf_tensor("wqb", [128, FT * 512], BF16))
        wkb = ps.enter_context(nc.sbuf_tensor("wkb", [128, FT * 512], BF16))
        wvb = ps.enter_context(nc.sbuf_tensor("wvb", [128, FT * 512], BF16))
        wbb = ps.enter_context(nc.sbuf_tensor("wbb", [128, FT * 4], BF16))
        xg = ps.enter_context(nc.sbuf_tensor("xg", [128, 2 * FT * 512], BF16))
        tabr = ps.enter_context(nc.sbuf_tensor("tabr", [128, 2 * 1024], BF16))
        qtmp = ps.enter_context(nc.sbuf_tensor("qtmp", [128, 512], BF16))
        ktmp = ps.enter_context(nc.sbuf_tensor("ktmp", [128, 512], BF16))
        knu = ps.enter_context(nc.sbuf_tensor("knu", [128, 512], BF16))
        scr = ps.enter_context(nc.sbuf_tensor("pscr", [128, 512], F32))
        brow = ps.enter_context(nc.sbuf_tensor("brow", [4, 512], F32))
        brow2 = ps.enter_context(nc.sbuf_tensor("brow2", [4, 512], F32))
        rowb = ps.enter_context(nc.sbuf_tensor("prow", [1, 1024], F32))

        xg_ready = {}
        tab_ready = {}

        def load_tc(sy, tc):
            s = tc % 2
            pc, half = tc // 2, tc % 2
            src = bass.AP(ctx.cc2_out, (pc * 128) * FT * TOK + half * 512,
                          [[FT * TOK, 128], [TOK, FT], [1, 512]])
            dst = xg[:, s * FT * 512:(s + 1) * FT * 512].rearrange(
                "p (k c) -> p k c", k=FT)
            opx[s].inc(sy.dma_start(out=dst, in_=src), 16)
            xg_ready[tc] = (s, opx[s].n)
            tld[s].inc(sy.dma_start(out=tabr[:, s * 1024:s * 1024 + 512],
                                    in_=I["rope_a"][:, tc * 512:(tc + 1) * 512]), 16)
            tld[s].inc(sy.dma_start(out=tabr[:, s * 1024 + 512:s * 1024 + 1024],
                                    in_=I["rope_b"][:, tc * 512:(tc + 1) * 512]), 16)
            tab_ready[tc] = (s, tld[s].n)

        with nc.Block() as blk:
            @blk.gpsimd
            def _(g):
                scg.inc(g.dma_start(
                    out=wqb[:, :].rearrange("p (k c) -> p k c", k=FT),
                    in_=I["Wq_sl"].rearrange("(k p) c -> p k c", k=FT)), 16)
                scg.inc(g.dma_start(
                    out=wkb[:, :].rearrange("p (k c) -> p k c", k=FT),
                    in_=I["Wk_sl"].rearrange("(k p) c -> p k c", k=FT)), 16)
                scg.inc(g.dma_start(
                    out=wvb[:, :].rearrange("p (k c) -> p k c", k=FT),
                    in_=I["Wv_sl"].rearrange("(k p) c -> p k c", k=FT)), 16)
                scg.inc(g.dma_start(
                    out=wbb[:, :].rearrange("p (k c) -> p k c", k=FT),
                    in_=I["Wb_sl"].rearrange("(k p) c -> p k c", k=FT)), 16)
                scg.inc(g.dma_start(out=ctx.maskb[:, :], in_=I["mask_attn"]), 16)
                g.wait_ge(scg.h, scg.n)

            @blk.sync
            def _(sy):
                sy.wait_ge(scc.h, ctx.cc2_ready)
                load_tc(sy, 0)

        for tc in range(8):
            bt, bv, bs = st.n, sv.n, ss.n
            bopx = xg_ready[tc][1]
            btab = tab_ready[tc][1]
            xs = (tc % 2) * FT * 512
            ts_ = (tc % 2) * 1024
            tA = tabr[:, ts_:ts_ + 512]
            tB = tabr[:, ts_ + 512:ts_ + 1024]
            with nc.Block() as blk:
                if tc < 7:
                    @blk.sync
                    def _(sy, tc=tc):
                        load_tc(sy, tc + 1)

                @blk.tensor
                def _(t, tc=tc, bt=bt, bv=bv, bs=bs, bopx=bopx, xs=xs):
                    t.wait_ge(opx[tc % 2].h, bopx)
                    for k in range(FT):
                        ins = t.matmul(psb[3][0:4, 0:512],
                                       wbb[:, k * 4:(k + 1) * 4],
                                       xg[:, xs + k * 512:xs + (k + 1) * 512],
                                       start=(k == 0), stop=(k == FT - 1))
                    st.inc(ins)
                    for p in range(NPAIR):
                        if p > 0:
                            t.wait_ge(ss.h, bs + 2 + 5 * (p - 1))
                        for k in range(FT):
                            ins = t.matmul(psb[0][:, 0:512],
                                           wqb[:, k * 512 + p * 128:k * 512 + p * 128 + 128],
                                           xg[:, xs + k * 512:xs + (k + 1) * 512],
                                           start=(k == 0), stop=(k == FT - 1))
                        st.inc(ins)
                        t.wait_ge(ss.h, bs + 2 + 5 * p)
                        if p > 0:
                            t.wait_ge(sv.h, bv + 8 * p)
                        st.inc(t.matmul(psb[4][:, 0:512], cbf[:, CB_SWAP:CB_SWAP + 128],
                                        qtmp[:, :], start=True, stop=True))
                        if p > 0:
                            t.wait_ge(ss.h, bs + 3 + 5 * (p - 1))
                        for k in range(FT):
                            ins = t.matmul(psb[1][:, 0:512],
                                           wkb[:, k * 512 + p * 128:k * 512 + p * 128 + 128],
                                           xg[:, xs + k * 512:xs + (k + 1) * 512],
                                           start=(k == 0), stop=(k == FT - 1))
                        st.inc(ins)
                        t.wait_ge(sv.h, bv + 8 * p + 2)
                        if p == 0:
                            t.wait_ge(ss.h, bs + 1)
                        else:
                            t.wait_ge(ss.h, bs + 4 + 5 * (p - 1))
                        st.inc(t.matmul(psb[3][32:33, 0:512], cbf[:, CB_ONECOL:CB_ONECOL + 1],
                                        knu[:, :], start=True, stop=True))
                        t.wait_ge(ss.h, bs + 5 + 5 * p)
                        if p > 0:
                            t.wait_ge(sv.h, bv + 8 * (p - 1) + 6)
                        st.inc(t.matmul(psb[5][:, 0:512],
                                        consts[0:1, C_ONEROW:C_ONEROW + 128],
                                        rowb[0:1, 512:1024], start=True, stop=True))
                        t.wait_ge(sv.h, bv + 8 * p + 6)
                        st.inc(t.matmul(psb[4][:, 0:512], cbf[:, CB_SWAP:CB_SWAP + 128],
                                        knu[:, :], start=True, stop=True))
                        if p > 0:
                            t.wait_ge(ss.h, bs + 6 + 5 * (p - 1))
                        for k in range(FT):
                            ins = t.matmul(psb[2][:, 0:512],
                                           wvb[:, k * 512 + p * 128:k * 512 + p * 128 + 128],
                                           xg[:, xs + k * 512:xs + (k + 1) * 512],
                                           start=(k == 0), stop=(k == FT - 1))
                        st.inc(ins)
                    t.wait_ge(sv.h, bv + 30)
                    t.wait_ge(sv.h, bv + 1)
                    for s2 in range(4):
                        ins = t.transpose(psb[5][:, s2 * 4:s2 * 4 + 4],
                                          brow2[0:4, s2 * 128:(s2 + 1) * 128],
                                          consts[0:4, C_IDENT:C_IDENT + 4])
                    st.inc(ins)

                @blk.scalar
                def _(sc, tc=tc, bt=bt, bv=bv, bs=bs):
                    sc.wait_ge(st.h, bt + 1)
                    ss.inc(sc.activation(out=brow[0:4, 0:512], in_=psb[3][0:4, 0:512],
                                         func=AF.Sigmoid))
                    for p in range(NPAIR):
                        sc.wait_ge(st.h, bt + 2 + 7 * p)
                        if p > 0:
                            sc.wait_ge(sv.h, bv + 8 * (p - 1) + 5)
                        ss.inc(sc.activation(out=qtmp[:, :], in_=psb[0][:, 0:512],
                                             func=AF.Copy))
                        sc.wait_ge(st.h, bt + 4 + 7 * p)
                        if p > 0:
                            sc.wait_ge(sv.h, bv + 8 * (p - 1) + 9)
                        ss.inc(sc.activation(out=ktmp[:, :], in_=psb[1][:, 0:512],
                                             func=AF.Copy))
                        sc.wait_ge(st.h, bt + 5 + 7 * p)
                        ss.inc(sc.activation(out=rowb[0:1, 0:512],
                                             in_=psb[3][32:33, 0:512], func=AF.Ln,
                                             bias=consts[0:1, C_MISC + 1:C_MISC + 2]))
                        sc.wait_ge(ss.h, bs + 4 + 5 * p)
                        ss.inc(sc.activation(out=rowb[0:1, 512:1024],
                                             in_=rowb[0:1, 0:512], func=AF.Exp,
                                             scale=-0.5))
                        sc.wait_ge(st.h, bt + 8 + 7 * p)
                        ss.inc(sc.activation(out=ctx.vfm[:, p * SEQ + tc * 512:p * SEQ + tc * 512 + 512],
                                             in_=psb[2][:, 0:512], func=AF.Copy))

                @blk.vector
                def _(v, tc=tc, bt=bt, bv=bv, bs=bs, btab=btab, tA=tA, tB=tB):
                    v.wait_ge(ss.h, bs + 1)
                    sv.inc(v.tensor_scalar(out=brow2[0:4, 0:512], in0=brow[0:4, 0:512],
                                           scalar1=2.0, scalar2=None, op0=ALU.mult))
                    v.wait_ge(tld[tc % 2].h, btab)
                    for p in range(NPAIR):
                        qsl = ctx.qfm[:, p * SEQ + tc * 512:p * SEQ + tc * 512 + 512]
                        ksl = ctx.knfm[:, p * SEQ + tc * 512:p * SEQ + tc * 512 + 512]
                        v.wait_ge(ss.h, bs + 3 + 5 * p)
                        if p > 0:
                            v.wait_ge(st.h, bt + 5 + 7 * (p - 1))
                        sv.inc(v.tensor_mul(knu[:, :], ktmp[:, :], ktmp[:, :]))
                        v.wait_ge(ss.h, bs + 2 + 5 * p)
                        sv.inc(v.tensor_mul(scr[:, :], qtmp[:, :], tA))
                        v.wait_ge(st.h, bt + 3 + 7 * p)
                        v.wait_ge(sv.h, sv.n)
                        sv.inc(v.tensor_mul(qtmp[:, :], psb[4][:, 0:512], tB))
                        v.wait_ge(sv.h, sv.n)
                        sv.inc(v.tensor_add(qsl, scr[:, :], qtmp[:, :]))
                        v.wait_ge(st.h, bt + 6 + 7 * p)
                        v.wait_ge(sv.h, sv.n)
                        sv.inc(v.tensor_mul(knu[:, :], ktmp[:, :], psb[5][:, 0:512]))
                        v.wait_ge(sv.h, sv.n)
                        sv.inc(v.tensor_mul(scr[:, :], knu[:, :], tA))
                        v.wait_ge(st.h, bt + 7 + 7 * p)
                        sv.inc(v.tensor_mul(ktmp[:, :], psb[4][:, 0:512], tB))
                        v.wait_ge(sv.h, sv.n)
                        sv.inc(v.tensor_add(ksl, scr[:, :], ktmp[:, :]))
                    v.wait_ge(st.h, bt + 30)
                    sv.inc(v.tensor_copy(ctx.btT[:, tc * 16:tc * 16 + 16],
                                         psb[5][:, 0:16]))
                    v.wait_ge(sv.h, sv.n)
                    sv.inc(v.tensor_scalar(out=ctx.btTs[:, tc * 16:tc * 16 + 16],
                                           in0=ctx.btT[:, tc * 16:tc * 16 + 16],
                                           scalar1=0.08838834764831845,
                                           scalar2=None, op0=ALU.mult))





def kernel(**inputs):
    in_maps = _host_prep(inputs)
    nc, ctx = build_program()
    res = run_bass_kernel_spmd(nc, in_maps, core_ids=list(range(8)))
    full = np.zeros((B, SEQ, DM), dtype=np.float32)
    for c in range(8):
        b = c // 4
        t0 = (c % 4) * TOK
        o = res.results[c]["out"]
        xf = o.reshape(128, FT, TOK)
        full[b, t0:t0 + TOK, :] = xf.transpose(2, 1, 0).reshape(TOK, DM)
    return full
